# version 04 (final)
# baseline (speedup 1.0000x reference)
import numpy as np
from contextlib import ExitStack

import concourse.bass as bass
import concourse.mybir as mybir
from concourse.bass_utils import run_bass_kernel_spmd

F32 = mybir.dt.float32
BF16 = mybir.dt.bfloat16
AF = mybir.ActivationFunctionType
ALU = mybir.AluOpType
AX = mybir.AxisListType

NCORES = 8
B = 2
S = 8192
D = 1024
STRIPE = 512
NSTRIPE = 16
T = 2048
NTT = T // 128
NFC = D // 128
LN_EPS = 1e-5



def core_stripes(c):
    return (c, NSTRIPE - 1 - c)


def shard_tokens(full):
    out = []
    for c in range(NCORES):
        parts = []
        for st in core_stripes(c):
            for b in range(B):
                parts.append(full[b, st * STRIPE:(st + 1) * STRIPE])
        out.append(np.ascontiguousarray(np.concatenate(parts, axis=0)))
    return out


def unshard_tokens(shards):
    tail = shards[0].shape[1:]
    full = np.empty((B, S) + tail, dtype=shards[0].dtype)
    for c in range(NCORES):
        sh = shards[c]
        i = 0
        for st in core_stripes(c):
            for b in range(B):
                full[b, st * STRIPE:(st + 1) * STRIPE] = sh[i:i + STRIPE]
                i += STRIPE
    return full


def global_positions(c):
    pos = np.empty(T, dtype=np.int64)
    i = 0
    for st in core_stripes(c):
        for b in range(B):
            pos[i:i + STRIPE] = np.arange(st * STRIPE, (st + 1) * STRIPE)
            i += STRIPE
    return pos




class Sem:
    __slots__ = ("h", "name", "v")

    def __init__(self, handle, name):
        self.h = handle
        self.name = name
        self.v = 0


class CallRec:
    __slots__ = ("attr", "args", "kwargs", "incs")

    def __init__(self, attr, args, kwargs):
        self.attr = attr
        self.args = args
        self.kwargs = kwargs
        self.incs = []

    def then_inc(self, sem, n=1):
        sem.v += n
        self.incs.append((sem, n))
        return self

    def inc_default(self, sem):
        sem.v += 1
        self.incs.append((sem, None))
        return self


class EngRec:
    __slots__ = ("_prog", "_name")

    def __init__(self, prog, name):
        self._prog = prog
        self._name = name

    def wait(self, sem, v=None):
        rec = CallRec("wait_ge", (("SEMH", sem), sem.v if v is None else v), {})
        self._prog.events.append((self._name, rec))

    def __getattr__(self, attr):
        prog = self._prog
        name = self._name

        def call(*args, **kwargs):
            rec = CallRec(attr, args, kwargs)
            prog.events.append((name, rec))
            return rec
        return call


class Prog:
    def __init__(self, nc):
        self.nc = nc
        self.es = ExitStack()
        self.events = []
        self._semn = 0
        self.t = EngRec(self, "tensor")
        self.v = EngRec(self, "vector")
        self.s = EngRec(self, "scalar")
        self.g = EngRec(self, "gpsimd")
        self.q = EngRec(self, "sync")

    def sem(self, name=None):
        self._semn += 1
        h = self.es.enter_context(self.nc.semaphore(name or f"sm{self._semn}"))
        return Sem(h, name or f"sm{self._semn}")

    def sbuf(self, name, shape, dtype):
        return self.es.enter_context(self.nc.sbuf_tensor(name, shape, dtype))

    def psum(self, name, shape, dtype=F32):
        return self.es.enter_context(self.nc.psum_tensor(name, shape, dtype))

    def finalize(self):
        from concourse.aot_env import allow_ici_all_to_all

        def replay(h, ename):
            for eng, rec in self.events:
                if eng != ename:
                    continue
                args = [a[1].h if isinstance(a, tuple) and len(a) == 2
                        and a[0] == "SEMH" else a for a in rec.args]
                r = getattr(h, rec.attr)(*args, **rec.kwargs)
                for sem, n in rec.incs:
                    r = r.then_inc(sem.h) if n is None else r.then_inc(sem.h, n)

        with allow_ici_all_to_all(), self.nc.Block() as block:
            @block.sync
            def _(h):
                replay(h, "sync")

            @block.tensor
            def _(h):
                replay(h, "tensor")

            @block.vector
            def _(h):
                replay(h, "vector")

            @block.scalar
            def _(h):
                replay(h, "scalar")

            @block.gpsimd
            def _(h):
                replay(h, "gpsimd")
        self.es.close()
        return self.nc


def as_ap(x):
    try:
        a = x.ap
        if callable(a):
            return x.ap()
    except Exception:
        pass
    return x


def fm_ap(buf, ch, c0, c1, tcols=T):
    return buf[:, ch * tcols + c0: ch * tcols + c1]




def emit_stats_tail(p, st, dve, act, ntiles, df, sum_done_v,
                    also_vare=None, eps=LN_EPS):
    p.s.wait(dve, sum_done_v)
    p.s.mul(st["mu"][:, :ntiles], st["sum"][:, :ntiles],
            1.0 / df).then_inc(act)
    p.v.wait(act)
    p.v.tensor_mul(st["tmp"][:, :ntiles], st["mu"][:, :ntiles],
                   st["mu"][:, :ntiles]).then_inc(dve)
    p.s.wait(dve)
    p.s.activation(st["tmp2"][:, :ntiles], st["tmp"][:, :ntiles], AF.Copy,
                   scale=-1.0, bias=eps).then_inc(act)
    p.v.wait(act)
    r = p.v.scalar_tensor_tensor(st["vare"][:, :ntiles], st["ssq"][:, :ntiles],
                                 1.0 / df, st["tmp2"][:, :ntiles],
                                 ALU.mult, ALU.add)
    if also_vare is not None:
        r = p.v.tensor_copy(also_vare[:, :ntiles], st["vare"][:, :ntiles])
    r.then_inc(dve)
    p.s.wait(dve)
    p.s.activation(st["lnv"][:, :ntiles], st["vare"][:, :ntiles], AF.Ln)
    p.s.activation(st["rstd"][:, :ntiles], st["lnv"][:, :ntiles], AF.Exp,
                   scale=-0.5).then_inc(act)


def emit_ln(p, x_tm, hn_tm, st, sem_dve, sem_act, wait_x=None,
            ntiles=NTT, df=D, also_vare=None, out_f32=None):
    if wait_x is not None:
        p.v.wait(*wait_x)
        p.s.wait(*wait_x)
    r = None
    for tt in range(ntiles):
        xs = x_tm[:, tt * df:(tt + 1) * df]
        r = p.v.reduce_sum(st["sum"][:, tt:tt + 1], xs, axis=AX.X)
        p.s.activation(st["sq" if tt % 2 == 0 else "sq2"][:, :df], xs,
                       AF.Square, accum_out=st["ssq"][:, tt:tt + 1])
    r.then_inc(sem_dve)
    emit_stats_tail(p, st, sem_dve, sem_act, ntiles, df, sem_dve.v,
                    also_vare=also_vare)
    p.v.wait(sem_act)
    for tt in range(ntiles):
        r = p.v.tensor_scalar(hn_tm[:, tt * df:(tt + 1) * df],
                              x_tm[:, tt * df:(tt + 1) * df],
                              st["mu"][:, tt:tt + 1], st["rstd"][:, tt:tt + 1],
                              ALU.subtract, ALU.mult)
        if out_f32 is not None:
            r = p.v.tensor_scalar(out_f32[:, tt * df:(tt + 1) * df],
                                  x_tm[:, tt * df:(tt + 1) * df],
                                  st["mu"][:, tt:tt + 1],
                                  st["rstd"][:, tt:tt + 1],
                                  ALU.subtract, ALU.mult)
    r.then_inc(sem_dve)
    return sem_dve.v


def mk_stats(p, tag, ntiles=NTT, df=D):
    return {
        "sum": p.sbuf(f"{tag}_sum", [128, ntiles], F32),
        "ssq": p.sbuf(f"{tag}_ssq", [128, ntiles], F32),
        "mu": p.sbuf(f"{tag}_mu", [128, ntiles], F32),
        "rstd": p.sbuf(f"{tag}_rstd", [128, ntiles], F32),
        "tmp": p.sbuf(f"{tag}_tmp", [128, ntiles], F32),
        "tmp2": p.sbuf(f"{tag}_tmp2", [128, ntiles], F32),
        "vare": p.sbuf(f"{tag}_vare", [128, ntiles], F32),
        "lnv": p.sbuf(f"{tag}_lnv", [128, ntiles], F32),
        "sq": p.sbuf(f"{tag}_sq", [128, df], F32),
        "sq2": p.sbuf(f"{tag}_sq2", [128, df], F32),
    }


def emit_transpose(p, src_tm, dst_fm, ident, pt2, sem_pe, sem_ev,
                   wait_src, ntiles=NTT, df=D, tcols=T, evac_eng="scalar",
                   tt_off=0):
    nch = df // 128
    ev = p.s if evac_eng == "scalar" else p.v
    p.t.wait(*wait_src)
    base_ev = sem_ev.v
    for tt in range(ntiles):
        ptile = pt2[tt % 2]
        if tt >= 2:
            p.t.wait(sem_ev, base_ev + tt - 1)
        r = None
        for ch in range(nch):
            r = p.t.transpose(ptile[:, ch * 128:(ch + 1) * 128],
                              src_tm[:, tt * df + ch * 128:tt * df + (ch + 1) * 128],
                              ident[:, :])
        r.then_inc(sem_pe)
        ev.wait(sem_pe)
        dstc = dst_fm.ap().shape[1] // tcols
        dst = dst_fm.ap().rearrange("p (c k) -> p c k", c=dstc)
        dst = dst[:, :nch, (tt_off + tt) * 128:(tt_off + tt + 1) * 128]
        src = ptile.ap().rearrange("p (c k) -> p c k", c=nch)
        if evac_eng == "scalar":
            ev.copy(dst, src).then_inc(sem_ev)
        else:
            ev.tensor_copy(dst, src).then_inc(sem_ev)
    return sem_ev.v


def emit_mm_tm(p, psum, act_fm, w_fm, tt, nch, wcol0, nout, sem_mm=None,
               acc=False, act_cols=T):
    wstride = as_ap(w_fm).shape[1] // nch
    r = None
    for j0 in range(0, nout, 512):
        j1 = min(nout, j0 + 512)
        for ch in range(nch):
            r = p.t.matmul(psum[:, j0:j1],
                           fm_ap(act_fm, ch, tt * 128, (tt + 1) * 128, act_cols),
                           w_fm[:, ch * wstride + wcol0 + j0:
                                ch * wstride + wcol0 + j1],
                           start=(ch == 0 and not acc), stop=(ch == nch - 1),
                           skip_group_check=acc)
    if sem_mm is not None:
        r.then_inc(sem_mm)
    return r


def emit_mm_fm(p, psum, w_fm, och, act_fm, c0, c1, nch, sem_mm=None,
               acc=False, act_cols=T, nout_chunks=None):
    nout = as_ap(w_fm).shape[1] // nch
    r = None
    for ch in range(nch):
        r = p.t.matmul(psum[:, :c1 - c0],
                       w_fm[:, ch * nout + och * 128: ch * nout + (och + 1) * 128],
                       fm_ap(act_fm, ch, c0, c1, act_cols),
                       start=(ch == 0 and not acc), stop=(ch == nch - 1),
                       skip_group_check=acc)
    if sem_mm is not None:
        r.then_inc(sem_mm)
    return r


def dma_w_fm(p, eng, sbuf_w, dram_w, din, dout, sem, col0=0, ncols=None):
    ncols = dout if ncols is None else ncols
    nch = din // 128
    src = dram_w.rearrange("(c p) j -> p c j", p=128)[:, :, col0:col0 + ncols]
    dst = as_ap(sbuf_w).rearrange("p (c j) -> p c j", c=nch)[:, :, :ncols]
    eng.dma_start(out=dst, in_=src).then_inc(sem, 16)
    return sem.v




def emit_stream_ln(p, xin, st, sems, tag, xt2=None, ntiles=NTT, df=D,
                   also_vare=None):
    xldS, dve, act = sems["xldS"], sems["dve"], sems["act"]
    if xt2 is None:
        xt2 = [p.sbuf(f"{tag}_x0", [128, df], F32),
               p.sbuf(f"{tag}_x1", [128, df], F32)]
    dve0, act0 = dve.v, act.v
    for tt in range(ntiles):
        sl = xldS[tt % 2]
        if tt >= 2:
            p.q.wait(dve, dve0 + tt - 1)
            p.q.wait(act, act0 + tt - 1)
        p.q.dma_start(out=xt2[tt % 2][:, :], in_=xin[tt * 128:(tt + 1) * 128, :]
                      ).then_inc(sl, 16)
        p.v.wait(sl, sl.v)
        p.s.wait(sl, sl.v)
        p.v.reduce_sum(st["sum"][:, tt:tt + 1], xt2[tt % 2][:, :],
                       axis=AX.X).then_inc(dve)
        p.s.activation(st["sq" if tt % 2 == 0 else "sq2"][:, :df],
                       xt2[tt % 2][:, :], AF.Square,
                       accum_out=st["ssq"][:, tt:tt + 1]).then_inc(act)
    emit_stats_tail(p, st, dve, act, ntiles, df, dve0 + ntiles,
                    also_vare=also_vare)
    return xt2, act.v


def emit_stream_norm_transpose(p, xin, st, hn_fm, ident, xt2, pt2, sems,
                               rstd_ready, ntiles=NTT, df=D, tcols=T,
                               gamma=None, beta=None, hn2=None):
    xldS, dve, act, pe, ev = (sems[k] for k in
                              ("xldS", "dve", "act", "pe", "ev"))
    nch = df // 128
    if hn2 is None:
        p._sntn = getattr(p, "_sntn", 0) + 1
        hn2 = [p.sbuf(f"snt{p._sntn}_h0", [128, df], BF16),
               p.sbuf(f"snt{p._sntn}_h1", [128, df], BF16)]
    p._last_hn2 = hn2
    dve0, pe0, ev0 = dve.v, pe.v, ev.v
    p.v.wait(sems["act"], rstd_ready)
    p.q.wait(sems["act"], rstd_ready)
    for tt in range(ntiles):
        sl = xldS[tt % 2]
        if tt >= 2:
            p.q.wait(dve, dve0 + tt - 1)
        p.q.dma_start(out=xt2[tt % 2][:, :],
                      in_=xin[tt * 128:(tt + 1) * 128, :]).then_inc(sl, 16)
        p.v.wait(sl, sl.v)
        if tt >= 2:
            p.v.wait(pe, pe0 + tt - 1)
        p.v.tensor_scalar(hn2[tt % 2][:, :], xt2[tt % 2][:, :],
                          st["mu"][:, tt:tt + 1], st["rstd"][:, tt:tt + 1],
                          ALU.subtract, ALU.mult).then_inc(dve)
        p.t.wait(dve, dve0 + tt + 1)
        if tt >= 2:
            p.t.wait(ev, ev0 + tt - 1)
        r = None
        for ch in range(nch):
            r = p.t.transpose(pt2[tt % 2][:, ch * 128:(ch + 1) * 128],
                              hn2[tt % 2][:, ch * 128:(ch + 1) * 128],
                              ident[:, :])
        r.then_inc(pe)
        p.v.wait(pe, pe0 + tt + 1)
        dstc = hn_fm.ap().shape[1] // tcols
        dst = hn_fm.ap().rearrange("p (c k) -> p c k", c=dstc)
        dst = dst[:, :nch, tt * 128:(tt + 1) * 128]
        src = as_ap(pt2[tt % 2]).rearrange("p (c k) -> p c k", c=nch)[:, :nch, :]
        if gamma is not None:
            r = None
            for ch in range(nch):
                r = p.v.tensor_scalar(
                    dst[:, ch, :], src[:, ch, :],
                    gamma[:, ch:ch + 1], beta[:, ch:ch + 1],
                    ALU.mult, ALU.add)
            r.then_inc(ev)
        else:
            p.v.tensor_copy(dst, src).then_inc(ev)
    return ev.v




def emit_l1ffn(p, xin, xout, W, ident, sems, tag="lf"):
    DH = 4096
    NHC = DH // 128
    G = 512
    NG = T // G
    xld, wld, dve, act, pe, ev, mm, sto = (
        sems[k] for k in ("xld", "wld", "dve", "act", "pe", "ev", "mm", "sto"))

    st = mk_stats(p, tag)
    hn_fm = p.sbuf(tag + "_hnfm", [128, NFC * T], BF16)
    xt2, rstd_ready = emit_stream_ln(p, xin, st, sems, tag)
    pt2 = [p.psum(tag + "_pt0", [128, D], BF16),
           p.psum(tag + "_pt1", [128, D], BF16)]
    emit_stream_norm_transpose(p, xin, st, hn_fm, ident, xt2, pt2, sems,
                               rstd_ready)

    A = p.sbuf(tag + "_A", [128, NHC * G], BF16)
    Bb = p.sbuf(tag + "_B", [128, NHC * G], BF16)
    y_fm = p.sbuf(tag + "_y", [128, NFC * G], BF16)
    wb1 = p.sbuf(tag + "_wb1", [128, NFC * 512], BF16)
    wb2 = p.sbuf(tag + "_wb2", [128, NFC * 512], BF16)
    whb = p.sbuf(tag + "_whb", [128, NHC * 128], BF16)
    ps_a = p.psum(tag + "_psa", [128, 512])
    ps_b = p.psum(tag + "_psb", [128, 512])
    ps_c = p.psum(tag + "_psc", [128, 512])
    tmp_sil = p.sbuf(tag + "_sil", [128, 512], BF16)
    tmp_lin = p.sbuf(tag + "_lin", [128, 512], BF16)

    for g in range(NG):
        c0, c1 = g * G, (g + 1) * G
        for hcg in range(NHC // 4):
            p.g.wait(mm, mm.v)
            dma_w_fm(p, p.g, wb1, W["w1T"], D, DH, sems["wlS"][0],
                     col0=hcg * 512, ncols=512)
            dma_w_fm(p, p.g, wb2, W["w2T"], D, DH, sems["wlS"][1],
                     col0=hcg * 512, ncols=512)
            p.t.wait(sems["wlS"][0], sems["wlS"][0].v)
            p.t.wait(sems["wlS"][1], sems["wlS"][1].v)
            for j in range(4):
                hc = hcg * 4 + j
                for c in range(c0, c1, 512):
                    emit_mm_fm(p, ps_a, wb1, j, hn_fm, c, c + 512, NFC,
                               sem_mm=mm)
                    emit_mm_fm(p, ps_b, wb2, j, hn_fm, c, c + 512, NFC,
                               sem_mm=mm)
                    p.s.wait(mm, mm.v)
                    p.s.activation(tmp_sil[:, :], ps_a[:, :512], AF.Silu,
                                   scale=W["s1"][:, hc:hc + 1])
                    p.s.activation(tmp_lin[:, :], ps_b[:, :512], AF.Copy,
                                   scale=W["s2"][:, hc:hc + 1]).then_inc(act)
                    p.v.wait(act, act.v)
                    p.v.tensor_mul(
                        fm_ap(A, hc, c - c0, c - c0 + 512, G),
                        tmp_sil[:, :], tmp_lin[:, :]).then_inc(dve)
                    p.t.wait(dve, dve.v)
        for (src, dst, wname, sname) in ((A, Bb, "h1T", "sh1"),
                                         (Bb, A, "h2T", "sh2")):
            for oc in range(NHC):
                p.g.wait(mm, mm.v)
                dma_w_fm(p, p.g, whb, W[wname], DH, DH, sems["wlS"][2],
                         col0=oc * 128, ncols=128)
                p.t.wait(sems["wlS"][2], sems["wlS"][2].v)
                for c in range(0, G, 512):
                    emit_mm_fm(p, ps_c, whb, 0, src, c, c + 512, NHC,
                               sem_mm=mm, act_cols=G)
                    p.s.wait(mm, mm.v)
                    p.s.activation(fm_ap(dst, oc, c, c + 512, G),
                                   ps_c[:, :512], AF.Gelu,
                                   scale=W[sname][:, oc:oc + 1]).then_inc(act)
                    p.t.wait(act, act.v)
        for oc in range(NFC):
            p.g.wait(mm, mm.v)
            dma_w_fm(p, p.g, whb, W["w3T"], DH, D, sems["wlS"][2],
                     col0=oc * 128, ncols=128)
            p.t.wait(sems["wlS"][2], sems["wlS"][2].v)
            for c in range(0, G, 512):
                emit_mm_fm(p, ps_c, whb, 0, A, c, c + 512, NHC,
                           sem_mm=mm, act_cols=G)
                p.s.wait(mm, mm.v)
                p.s.activation(fm_ap(y_fm, oc, c, c + 512, G),
                               ps_c[:, :512], AF.Copy,
                               scale=W["s3"][:, oc:oc + 1]).then_inc(act)
                p.t.wait(act, act.v)
        p.t.wait(act, act.v)
        p.q.wait(dve, dve.v)
        ev0, pe0, dve0 = ev.v, pe.v, dve.v
        for j in range(G // 128):
            tt = (c0 // 128) + j
            ptile = pt2[j % 2]
            if j >= 2:
                p.t.wait(ev, ev0 + j - 1)
            r = None
            for ch in range(NFC):
                r = p.t.transpose(ptile[:, ch * 128:(ch + 1) * 128],
                                  fm_ap(y_fm, ch, j * 128, (j + 1) * 128, G),
                                  ident[:, :])
            r.then_inc(pe)
            if j >= 2:
                p.q.wait(ev, ev0 + j - 1)
            sl = sems["xldS"][j % 2]
            p.q.dma_start(out=xt2[j % 2][:, :],
                          in_=xin[tt * 128:(tt + 1) * 128, :]).then_inc(sl, 16)
            p.v.wait(sl, sl.v)
            p.v.wait(pe, pe0 + j + 1)
            p.v.tensor_add(xt2[j % 2][:, :], xt2[j % 2][:, :],
                           ptile[:, :D]).then_inc(ev)
            p.q.wait(ev, ev.v)
            p.q.dma_start(out=xout[tt * 128:(tt + 1) * 128, :],
                          in_=xt2[j % 2][:, :]).then_inc(sto, 16)
    return sto.v




def mk_sems(p):
    d = {k: p.sem(k) for k in
         ("xld", "wld", "dve", "act", "pe", "ev", "mm", "sto", "ild", "sld")}
    d["xldS"] = [p.sem("xldS0"), p.sem("xldS1")]
    d["wlS"] = [p.sem("wlS0"), p.sem("wlS1"), p.sem("wlS2")]
    return d


def prep_l1ffn_host(inputs):
    W = {}
    W["l1f_w1T"] = np.ascontiguousarray(inputs["L1_ffn_w1_w"].T)
    W["l1f_w2T"] = np.ascontiguousarray(inputs["L1_ffn_w2_w"].T)
    W["l1f_h1T"] = np.ascontiguousarray(inputs["L1_ffn_h1_w"].T)
    W["l1f_h2T"] = np.ascontiguousarray(inputs["L1_ffn_h2_w"].T)
    W["l1f_w3T"] = np.ascontiguousarray(inputs["L1_ffn_w3_w"].T)
    W["l1f_s1"] = np.ascontiguousarray(inputs["L1_ffn_w1_s"].reshape(32, 128).T)
    W["l1f_s2"] = np.ascontiguousarray(inputs["L1_ffn_w2_s"].reshape(32, 128).T)
    W["l1f_sh1"] = np.ascontiguousarray(inputs["L1_ffn_h1_s"].reshape(32, 128).T)
    W["l1f_sh2"] = np.ascontiguousarray(inputs["L1_ffn_h2_s"].reshape(32, 128).T)
    W["l1f_s3"] = np.ascontiguousarray(inputs["L1_ffn_w3_s"].reshape(8, 128).T)
    W["ident"] = np.eye(128, dtype=np.float32)
    return W


def declare_l1ffn(p, nc):
    W = {}
    for nm, shp in (("l1f_w1T", [D, 4096]), ("l1f_w2T", [D, 4096]),
                    ("l1f_h1T", [4096, 4096]), ("l1f_h2T", [4096, 4096]),
                    ("l1f_w3T", [4096, D])):
        W[nm] = nc.dram_tensor(nm, shp, F32, kind="ExternalInput").ap()
    for nm, n in (("l1f_s1", 32), ("l1f_s2", 32), ("l1f_sh1", 32),
                  ("l1f_sh2", 32), ("l1f_s3", 8)):
        W[nm + "_d"] = nc.dram_tensor(nm, [128, n], F32,
                                      kind="ExternalInput").ap()
        W[nm] = p.sbuf(nm + "_s", [128, n], F32)
    return W


def emit_l1ffn_load_scales(p, W, sems):
    for nm in ("l1f_s1", "l1f_s2", "l1f_sh1", "l1f_sh2", "l1f_s3"):
        p.q.dma_start(out=W[nm][:, :], in_=W[nm + "_d"]).then_inc(sems["sld"], 16)
    p.s.wait(sems["sld"])
    p.v.wait(sems["sld"])


def load_ident(p, nc, sems):
    ident_d = nc.dram_tensor("ident", [128, 128], F32,
                             kind="ExternalInput").ap()
    ident = p.sbuf("ident_s", [128, 128], BF16)
    p.g.dma_start(out=ident[:, :], in_=ident_d).then_inc(sems["ild"], 16)
    p.t.wait(sems["ild"])
    return ident


def mk_nc():
    return bass.Bass("TRN2", target_bir_lowering=False,
                     detect_race_conditions=False)


def build_l1ffn_nc():
    nc = mk_nc()
    p = Prog(nc)
    xin = nc.dram_tensor("xin", [T, D], F32, kind="ExternalInput").ap()
    xout = nc.dram_tensor("xout", [T, D], F32, kind="ExternalOutput").ap()
    sems = mk_sems(p)
    W = declare_l1ffn(p, nc)
    emit_l1ffn_load_scales(p, W, sems)
    ident = load_ident(p, nc, sems)
    Wd = {"w1T": W["l1f_w1T"], "w2T": W["l1f_w2T"], "h1T": W["l1f_h1T"],
          "h2T": W["l1f_h2T"], "w3T": W["l1f_w3T"], "s1": W["l1f_s1"],
          "s2": W["l1f_s2"], "sh1": W["l1f_sh1"], "sh2": W["l1f_sh2"],
          "s3": W["l1f_s3"]}
    emit_l1ffn(p, xin, xout, Wd, ident, sems)
    p.q.wait(sems["sto"])
    return p.finalize()


def build_dbg_ln_nc():
    nc = mk_nc()
    p = Prog(nc)
    xin = nc.dram_tensor("xin", [T, D], F32, kind="ExternalInput").ap()
    hn_out = nc.dram_tensor("hn_out", [NFC * 128, T], F32,
                            kind="ExternalOutput").ap()
    mu_out = nc.dram_tensor("mu_out", [128, NTT], F32, kind="ExternalOutput").ap()
    rs_out = nc.dram_tensor("rs_out", [128, NTT], F32, kind="ExternalOutput").ap()
    sems = mk_sems(p)
    ident = load_ident(p, nc, sems)
    st = mk_stats(p, "dbg")
    hn_fm = p.sbuf("dbg_hnfm", [128, NFC * T], BF16)
    xt2, rstd_ready = emit_stream_ln(p, xin, st, sems, "dbg")
    pt2 = [p.psum("dbg_pt0", [128, D], BF16), p.psum("dbg_pt1", [128, D], BF16)]
    emit_stream_norm_transpose(p, xin, st, hn_fm, ident, xt2, pt2, sems,
                               rstd_ready)
    hn_f32 = p.sbuf("dbg_f32", [128, T], F32)
    p.q.wait(sems["ev"])
    for ch in range(NFC):
        p.v.wait(sems["ev"])
        if ch > 0:
            p.v.wait(sems["sto"])
        p.v.tensor_copy(hn_f32[:, :], fm_ap(hn_fm, ch, 0, T)).then_inc(sems["dve"])
        p.q.wait(sems["dve"])
        dst = hn_out[ch * 128:(ch + 1) * 128, :]
        p.q.dma_start(out=dst, in_=hn_f32[:, :]).then_inc(sems["sto"], 16)
    p.q.dma_start(out=mu_out, in_=st["mu"][:, :]).then_inc(sems["sto"], 16)
    p.q.dma_start(out=rs_out, in_=st["rstd"][:, :]).then_inc(sems["sto"], 16)
    p.q.wait(sems["sto"])
    return p.finalize()




def prep_moe_host(inputs):
    W = {}
    W["moe_WrT"] = np.ascontiguousarray(inputs["L0_ffn_Wr"])
    W["moe_br"] = np.ascontiguousarray(inputs["L0_ffn_br"].reshape(1, 16))
    W["moe_W1"] = np.ascontiguousarray(inputs["L0_ffn_W1"])
    W["moe_W2"] = np.ascontiguousarray(inputs["L0_ffn_W2"])
    W["moe_b1"] = np.ascontiguousarray(inputs["L0_ffn_b1"].reshape(1, 16 * 1024))
    W["moe_b2"] = np.ascontiguousarray(inputs["L0_ffn_b2"])
    W["moe_g1bc"] = np.ascontiguousarray(
        np.broadcast_to(inputs["L0_ffn_g1"][:, None, :], (16, 128, 1024)))
    W["moe_be1bc"] = np.ascontiguousarray(
        np.broadcast_to(inputs["L0_ffn_be1"][:, None, :], (16, 128, 1024)))
    return W


def declare_moe(p, nc):
    W = {}
    W["moe_WrT"] = nc.dram_tensor("moe_WrT", [D, 16], F32, kind="ExternalInput").ap()
    W["moe_br"] = nc.dram_tensor("moe_br", [1, 16], F32, kind="ExternalInput").ap()
    W["moe_W1"] = nc.dram_tensor("moe_W1", [16, D, 1024], F32, kind="ExternalInput").ap()
    W["moe_W2"] = nc.dram_tensor("moe_W2", [16, 1024, D], F32, kind="ExternalInput").ap()
    W["moe_b1"] = nc.dram_tensor("moe_b1", [1, 16 * 1024], F32, kind="ExternalInput").ap()
    W["moe_b2"] = nc.dram_tensor("moe_b2", [16, 1024], F32, kind="ExternalInput").ap()
    W["moe_g1bc"] = nc.dram_tensor("moe_g1bc", [16, 128, 1024], F32, kind="ExternalInput").ap()
    W["moe_be1bc"] = nc.dram_tensor("moe_be1bc", [16, 128, 1024], F32, kind="ExternalInput").ap()
    return W


def emit_moe(p, xin, xout, W, ident, sems, tag="moe", dbg=None):
    xldS, dve, act, pe, ev, mm, sto = (
        sems[k] for k in ("xldS", "dve", "act", "pe", "ev", "mm", "sto"))
    wl0, wl1, wl2 = sems["wlS"]

    st = mk_stats(p, tag)
    est = mk_stats(p, tag + "e", df=1)
    hn_fm = p.sbuf(tag + "_hnfm", [128, NFC * T], BF16)
    xt2, rstd_ready = emit_stream_ln(p, xin, st, sems, tag)
    ptA = p.psum(tag + "_ptA", [128, D], BF16)
    ptB = p.psum(tag + "_ptB", [128, D], BF16)
    emit_stream_norm_transpose(p, xin, st, hn_fm, ident, xt2, [ptA, ptB],
                               sems, rstd_ready)

    y_acc = p.sbuf(tag + "_yacc", [128, NTT * D], F32)
    h_tm = p.sbuf(tag + "_htm", [128, NTT * 1024], BF16)
    wbA = p.sbuf(tag + "_wbA", [128, 8 * 1024], BF16)
    wbB = p.sbuf(tag + "_wbB", [128, 8 * 1024], BF16)
    g1t = p.sbuf(tag + "_g1t", [128, 1024], BF16)
    be1t = p.sbuf(tag + "_be1t", [128, 1024], BF16)
    b1e = p.sbuf(tag + "_b1e", [1, 1024], BF16)
    brr = p.sbuf(tag + "_brr", [1, 16], BF16)
    b2t = p.sbuf(tag + "_b2t", [16, 1024], BF16)
    ones1 = p.sbuf(tag + "_ones1", [1, 128], BF16)
    wact = p.sbuf(tag + "_wact", [128, NTT * 16], F32)
    wdve = p.sbuf(tag + "_wdve", [128, NTT * 16], F32)
    lg = p.sbuf(tag + "_lg", [128, 16], F32)
    e1 = p.sbuf(tag + "_e1", [128, 16], F32)
    e2 = p.sbuf(tag + "_e2", [128, 16], F32)
    s_a = p.sbuf(tag + "_sa", [128, 1], F32)
    s_b = p.sbuf(tag + "_sb", [128, 1], F32)
    s_c = p.sbuf(tag + "_sc", [128, 1], F32)
    s_d = p.sbuf(tag + "_sd", [128, 1], F32)
    t1 = p.sbuf(tag + "_t1", [128, 1024], BF16)
    g_tm = p.sbuf(tag + "_gtm", [128, 1024], BF16)
    diag2 = [p.sbuf(tag + "_dg0", [128, 128], BF16),
             p.sbuf(tag + "_dg1", [128, 128], BF16)]
    g_fm = p.sbuf(tag + "_gfm", [128, 1024], BF16)

    ph = [p.psum(tag + "_ph0", [128, 1024]), p.psum(tag + "_ph1", [128, 1024])]
    py = p.psum(tag + "_py", [128, 1024])

    p.v.memset(ones1[:, :], 1.0).then_inc(dve)
    p.g.dma_start(out=brr[:, :], in_=W["moe_br"]).then_inc(wl0, 16)
    p.g.dma_start(out=b2t[:, :], in_=W["moe_b2"]).then_inc(wl0, 16)
    p.t.wait(wl0, wl0.v)
    p.t.wait(dve, dve.v)

    wrb = p.sbuf(tag + "_wrb", [128, NFC * 16], BF16)
    dma_w_fm(p, p.g, wrb, W["moe_WrT"], D, 16, wl1)
    p.t.wait(wl1, wl1.v)
    for tt in range(NTT):
        pr = ph[tt % 2]
        if tt >= 2:
            p.t.wait(act, act.v - 1)
        emit_mm_tm(p, pr[:, 0:16], hn_fm, wrb, tt, NFC, 0, 16)
        p.t.matmul(pr[:, 0:16], ones1[:, :], brr[:, :], start=False,
                   stop=True, skip_group_check=True).then_inc(mm)
        def _alt(fn_s=None, fn_v=None):
            if fn_s is not None:
                p.s.wait(dve, dve.v)
                fn_s().then_inc(act)
            else:
                p.v.wait(act, act.v)
                fn_v().then_inc(dve)
        wsl = wdve[:, tt * 16:(tt + 1) * 16]
        p.s.wait(mm, mm.v)
        p.s.copy(lg[:, :], pr[:, 0:16]).then_inc(act)
        _alt(fn_v=lambda: p.v.reduce_max(s_a[:, :], lg[:, :], axis=AX.X,
                                         negate=True))
        _alt(fn_s=lambda: p.s.activation(e1[:, :], lg[:, :], AF.Exp,
                                         bias=s_a[:, 0:1]))
        _alt(fn_v=lambda: p.v.reduce_max(s_b[:, :], e1[:, :], axis=AX.X))
        _alt(fn_s=lambda: p.s.copy(s_c[:, :], s_b[:, :]))
        _alt(fn_v=lambda: p.v.scalar_tensor_tensor(
            e2[:, :], e1[:, :], s_c[:, 0:1], e1[:, :], ALU.is_lt, ALU.mult))
        _alt(fn_s=lambda: p.s.copy(lg[:, :], e2[:, :]))
        _alt(fn_v=lambda: p.v.reduce_max(s_b[:, :], lg[:, :], axis=AX.X))
        _alt(fn_s=lambda: p.s.copy(s_d[:, :], s_b[:, :]))
        _alt(fn_v=lambda: p.v.tensor_add(s_a[:, :], s_c[:, :], s_d[:, :]))
        _alt(fn_s=lambda: p.s.copy(s_b[:, :], s_a[:, :]))
        _alt(fn_v=lambda: p.v.reciprocal(s_a[:, :], s_b[:, :]))
        _alt(fn_s=lambda: p.s.copy(s_b[:, :], s_a[:, :]))
        _alt(fn_v=lambda: p.v.tensor_scalar_mul(e2[:, :], e1[:, :],
                                                s_b[:, 0:1]))
        _alt(fn_s=lambda: p.s.copy(lg[:, :], e2[:, :]))
        _alt(fn_v=lambda: p.v.scalar_tensor_tensor(
            wsl, e1[:, :], s_d[:, 0:1], lg[:, :], ALU.is_ge, ALU.mult))
        _alt(fn_s=lambda: p.s.copy(wact[:, tt * 16:(tt + 1) * 16], wsl))

    if dbg is not None:
        p.q.wait(act, act.v)
        p.q.dma_start(out=dbg["wact"], in_=wact[:, :]).then_inc(sto, 16)
    p.v.wait(act, act.v)
    p.v.memset(y_acc[:, :], 0.0).then_inc(dve)

    for e in range(16):
        p.g.wait(mm, mm.v)
        src = W["moe_W1"][e].rearrange("(c p) j -> p c j", p=128)
        p.g.dma_start(out=wbA.ap().rearrange("p (c j) -> p c j", c=8),
                      in_=src).then_inc(wl1, 16)
        src = W["moe_W2"][e].rearrange("(c p) j -> p c j", p=128)
        p.g.dma_start(out=wbB.ap().rearrange("p (c j) -> p c j", c=8),
                      in_=src).then_inc(wl2, 16)
        p.g.dma_start(out=b1e[:, :],
                      in_=W["moe_b1"][0:1, e * 1024:(e + 1) * 1024]
                      ).then_inc(wl0, 16)
        p.g.dma_start(out=g1t[:, :], in_=W["moe_g1bc"][e]).then_inc(wl0, 16)
        p.g.dma_start(out=be1t[:, :], in_=W["moe_be1bc"][e]).then_inc(wl0, 16)
        p.t.wait(wl1, wl1.v)
        p.t.wait(wl0, wl0.v)
        est["sq"], est["sq2"] = st["sq"], st["sq2"]
        for tt in range(NTT):
            pr = ph[tt % 2]
            if tt >= 2:
                p.t.wait(act, act.v - 1)
            emit_mm_tm(p, pr, hn_fm, wbA, tt, NFC, 0, 1024)
            p.t.matmul(pr[:, 0:512], ones1[:, :],
                       b1e[0:1, 0:512], start=False,
                       stop=False, skip_group_check=True)
            p.t.matmul(pr[:, 512:1024], ones1[:, :],
                       b1e[0:1, 512:1024], start=False,
                       stop=True, skip_group_check=True).then_inc(mm)
            p.s.wait(mm, mm.v)
            p.s.activation(h_tm[:, tt * 1024:(tt + 1) * 1024], pr[:, :],
                           AF.Copy, accum_out=est["sum"][:, tt:tt + 1])
            p.s.activation(est["sq" if tt % 2 == 0 else "sq2"][:, :],
                           h_tm[:, tt * 1024:(tt + 1) * 1024], AF.Square,
                           accum_out=est["ssq"][:, tt:tt + 1]).then_inc(act)
        emit_stats_tail(p, est, dve, act, NTT, 1024, dve.v)
        rstd_v = act.v
        p.v.wait(act, rstd_v)
        p.v.wait(wl0, wl0.v)
        p.t.wait(wl2, wl2.v)
        for tt in range(NTT):
            p.v.wait(pe, pe.v)
            p.v.tensor_scalar(t1[:, :], h_tm[:, tt * 1024:(tt + 1) * 1024],
                              est["mu"][:, tt:tt + 1], est["rstd"][:, tt:tt + 1],
                              ALU.subtract, ALU.mult)
            p.v.tensor_mul(t1[:, :], t1[:, :], g1t[:, :])
            p.v.tensor_add(t1[:, :], t1[:, :], be1t[:, :]).then_inc(dve)
            p.s.wait(dve, dve.v)
            p.s.activation(g_tm[:, :], t1[:, :], AF.Gelu).then_inc(act)
            p.v.wait(act, act.v)
            p.v.tensor_scalar_mul(diag2[tt % 2][:, :], ident[:, :],
                                  wact[:, tt * 16 + e:tt * 16 + e + 1]
                                  ).then_inc(dve)
            p.t.wait(dve, dve.v)
            p.t.wait(act, act.v)
            r = None
            for ch in range(8):
                r = p.t.matmul(ph[tt % 2][:, ch * 128:(ch + 1) * 128],
                               g_tm[:, ch * 128:(ch + 1) * 128],
                               diag2[tt % 2][:, :], start=True, stop=True)
            r.then_inc(pe)
            p.v.wait(pe, pe.v)
            p.v.tensor_copy(g_fm[:, :], ph[tt % 2][:, :]).then_inc(dve)
            p.t.wait(dve, dve.v)
            r = None
            for ch in range(8):
                r = p.t.matmul(py[:, 0:512], g_fm[:, ch * 128:(ch + 1) * 128],
                               wbB[:, ch * 1024:ch * 1024 + 512],
                               start=(ch == 0), stop=(ch == 7))
            for ch in range(8):
                r = p.t.matmul(py[:, 512:1024],
                               g_fm[:, ch * 128:(ch + 1) * 128],
                               wbB[:, ch * 1024 + 512:(ch + 1) * 1024],
                               start=(ch == 0), stop=(ch == 7))
            r.then_inc(mm)
            p.v.wait(mm, mm.v)
            p.v.tensor_add(y_acc[:, tt * D:(tt + 1) * D],
                           y_acc[:, tt * D:(tt + 1) * D], py[:, :]).then_inc(dve)
            p.t.wait(dve, dve.v)
        if dbg is not None and e == 0:
            p.q.wait(dve, dve.v)
            p.g.wait(dve, dve.v)
            p.g.dma_start(out=dbg["htm0"], in_=h_tm[:, :]).then_inc(sto, 16)
            p.q.dma_start(out=dbg["yacc0"], in_=y_acc[:, :]).then_inc(sto, 16)
            p.q.dma_start(out=dbg["mu0"], in_=est["mu"][:, :]).then_inc(sto, 16)
            p.q.dma_start(out=dbg["rs0"], in_=est["rstd"][:, :]).then_inc(sto, 16)
    p.q.wait(dve, dve.v)
    for tt in range(NTT):
        p.v.wait(pe, pe.v)
        p.v.tensor_copy(diag2[0][:, 0:16],
                        wdve[:, tt * 16:(tt + 1) * 16]).then_inc(dve)
        p.t.wait(dve, dve.v)
        p.t.transpose(ptA[:, 0:128], diag2[0][:, :], ident[:, :]).then_inc(pe)
        p.v.wait(pe, pe.v)
        p.v.tensor_copy(g_fm[:, 0:128], ptA[:, 0:128]).then_inc(dve)
        p.t.wait(dve, dve.v)
        p.t.matmul(py[:, 0:512], g_fm[0:16, 0:128], b2t[:, 0:512],
                   start=True, stop=True)
        p.t.matmul(py[:, 512:1024], g_fm[0:16, 0:128], b2t[:, 512:1024],
                   start=True, stop=True).then_inc(mm)
        sl = xldS[tt % 2]
        if tt >= 2:
            p.q.wait(dve, dve.v - 1)
        p.q.dma_start(out=xt2[tt % 2][:, :],
                      in_=xin[tt * 128:(tt + 1) * 128, :]).then_inc(sl, 16)
        p.v.wait(sl, sl.v)
        p.v.wait(mm, mm.v)
        p.v.tensor_add(xt2[tt % 2][:, :], xt2[tt % 2][:, :],
                       y_acc[:, tt * D:(tt + 1) * D])
        p.v.tensor_add(xt2[tt % 2][:, :], xt2[tt % 2][:, :],
                       py[:, :]).then_inc(dve)
        p.t.wait(dve, dve.v)
        p.q.wait(dve, dve.v)
        p.q.dma_start(out=xout[tt * 128:(tt + 1) * 128, :],
                      in_=xt2[tt % 2][:, :]).then_inc(sto, 16)
    return sto.v


def build_moe_nc(dbg=False):
    nc = mk_nc()
    p = Prog(nc)
    xin = nc.dram_tensor("xin", [T, D], F32, kind="ExternalInput").ap()
    xout = nc.dram_tensor("xout", [T, D], F32, kind="ExternalOutput").ap()
    sems = mk_sems(p)
    W = declare_moe(p, nc)
    ident = load_ident(p, nc, sems)
    dd = None
    if dbg:
        dd = {"wact": nc.dram_tensor("wact", [128, NTT * 16], F32, kind="ExternalOutput").ap(),
              "htm0": nc.dram_tensor("htm0", [128, NTT * 1024], F32, kind="ExternalOutput").ap(),
              "yacc0": nc.dram_tensor("yacc0", [128, NTT * D], F32, kind="ExternalOutput").ap(),
              "mu0": nc.dram_tensor("mu0", [128, NTT], F32, kind="ExternalOutput").ap(),
              "rs0": nc.dram_tensor("rs0", [128, NTT], F32, kind="ExternalOutput").ap()}
    emit_moe(p, xin, xout, W, ident, sems, dbg=dd)
    p.q.wait(sems["sto"])
    return p.finalize()



MINF = -3.0e38


def prep_attnb_host(inputs):
    W = {"ab_wq": np.ascontiguousarray(inputs["L0_attn_b_Wq"]),
         "ab_wk": np.ascontiguousarray(inputs["L0_attn_b_Wk"]),
         "ab_wv": np.ascontiguousarray(inputs["L0_attn_b_Wv"]),
         "ab_wo": np.ascontiguousarray(inputs["L0_attn_b_Wo"])}
    m = np.zeros((4, 128, 512), np.float32)
    for o in range(4):
        q = np.arange(128)[:, None] + o * 128
        k = np.arange(512)[None, :]
        m[o] = np.where(k <= q, 0.0, MINF)
    W["ab_mask"] = m
    return W


def prep_attnb_kill(core):
    k = np.full((1, 32), -30000.0, np.float32)
    for qslot in range(2):
        sq = core_stripes(core)[qslot]
        for sk in range(NSTRIPE):
            if sk < sq:
                k[0, qslot * 16 + sk] = 0.0
    return k


def declare_attnb(nc):
    W = {}
    for nm in ("ab_wq", "ab_wk", "ab_wv", "ab_wo"):
        W[nm] = nc.dram_tensor(nm, [D, D], F32, kind="ExternalInput").ap()
    W["ab_mask"] = nc.dram_tensor("ab_mask", [4, 128, 512], F32,
                                  kind="ExternalInput").ap()
    W["ab_kill"] = nc.dram_tensor("ab_kill", [1, 32], F32,
                                  kind="ExternalInput").ap()
    return W


def emit_attnb(p, nc, xin, xout, W, ident, sems, tag="ab"):
    xldS, dve, act, pe, ev, mm, sto = (
        sems[k] for k in ("xldS", "dve", "act", "pe", "ev", "mm", "sto"))
    wl0, wl1, wl2 = sems["wlS"]
    cc = p.sem(tag + "_cc")
    kvs = p.sem(tag + "_kvs")
    kld = [p.sem(tag + "_kld0"), p.sem(tag + "_kld1")]
    vld = [p.sem(tag + "_vld0"), p.sem(tag + "_vld1")]
    scl = float(1.0 / np.sqrt(np.float32(D)))

    st = mk_stats(p, tag)
    hn_fm = p.sbuf(tag + "_hnfm", [128, NFC * T], BF16)
    xt2, rstd_ready = emit_stream_ln(p, xin, st, sems, tag)
    ptn = p.psum(tag + "_ptn", [128, 2 * D], BF16)
    emit_stream_norm_transpose(p, xin, st, hn_fm, ident, xt2,
                               [ptn[:, 0:D], ptn[:, D:2 * D]], sems,
                               rstd_ready)

    k_in = nc.dram_tensor(tag + "_kin", [D, T], BF16)
    v_in = nc.dram_tensor(tag + "_vin", [T, D], BF16)
    k_all = nc.dram_tensor(tag + "_kall", [NCORES * D, T], BF16)
    v_all = nc.dram_tensor(tag + "_vall", [NCORES * T, D], BF16)

    wb = p.sbuf(tag + "_wb", [128, NFC * D], BF16)
    k_fm = p.sbuf(tag + "_kfm", [128, NFC * T], BF16)
    q_fm = p.sbuf(tag + "_qfm", [128, NFC * T], BF16)
    vtile = p.sbuf(tag + "_vt", [128, 2 * D], BF16)
    ps_s = p.psum(tag + "_pss", [128, 512])
    ps_av = p.psum(tag + "_psav", [128, D])
    ptp = p.psum(tag + "_ptp", [128, 512], BF16)
    ptA = p.psum(tag + "_ptA", [128, D], BF16)
    ones1 = p.sbuf(tag + "_ones1", [1, 128], BF16)
    killr = p.sbuf(tag + "_killr", [1, 32], BF16)
    killb = p.sbuf(tag + "_killb", [128, 32], F32)
    p.v.memset(ones1[:, :], 1.0).then_inc(dve)
    p.g.dma_start(out=killr[:, :], in_=W["ab_kill"]).then_inc(wl2, 16)
    p.t.wait(wl2, wl2.v)
    p.t.wait(dve, dve.v)
    p.t.matmul(ps_s[:, 0:32], ones1[:, :], killr[:, :], start=True,
               stop=True).then_inc(mm)
    p.v.wait(mm, mm.v)
    p.v.tensor_copy(killb[:, :], ps_s[:, 0:32]).then_inc(dve)
    p.t.wait(dve, dve.v)

    dma_w_fm(p, p.g, wb, W["ab_wk"], D, D, wl0)
    p.t.wait(wl0, wl0.v)
    for och in range(NFC):
        for tg in range(4):
            emit_mm_fm(p, ps_s, wb, och, hn_fm, tg * 512, (tg + 1) * 512,
                       NFC, sem_mm=mm)
            p.s.wait(mm, mm.v)
            p.s.activation(fm_ap(k_fm, och, tg * 512, (tg + 1) * 512),
                           ps_s[:, :], AF.Copy).then_inc(act)
            p.t.wait(act, act.v)
    p.g.wait(act, act.v)
    p.g.dma_start(out=k_in.ap().rearrange("(c p) t -> p c t", p=128),
                  in_=k_fm.ap().rearrange("p (c t) -> p c t", c=NFC)
                  ).then_inc(kvs, 16)
    p.g.wait(mm, mm.v)
    dma_w_fm(p, p.g, wb, W["ab_wv"], D, D, wl0)
    p.t.wait(wl0, wl0.v)
    kvs0 = kvs.v
    for tt in range(NTT):
        emit_mm_tm(p, ps_av, hn_fm, wb, tt, NFC, 0, D, sem_mm=mm)
        p.s.wait(mm, mm.v)
        if tt >= 2:
            p.s.wait(kvs, kvs0 + 16 * (tt - 1))
        p.s.activation(vtile[:, (tt % 2) * D:(tt % 2 + 1) * D], ps_av[:, :],
                       AF.Copy).then_inc(act)
        p.t.wait(act, act.v)
        p.g.wait(act, act.v)
        p.g.dma_start(out=v_in[tt * 128:(tt + 1) * 128, :],
                      in_=vtile[:, (tt % 2) * D:(tt % 2 + 1) * D]
                      ).then_inc(kvs, 16)
    p.g.wait(kvs, kvs.v)
    p.g.collective_compute(
        "AllGather", ALU.bypass, replica_groups=[list(range(NCORES))],
        ins=[k_in.ap().opt()], outs=[k_all.ap().opt()]).inc_default(cc)
    p.g.collective_compute(
        "AllGather", ALU.bypass, replica_groups=[list(range(NCORES))],
        ins=[v_in.ap().opt()], outs=[v_all.ap().opt()]).inc_default(cc)
    p.g.wait(cc, cc.v)
    p.g.wait(mm, mm.v)
    dma_w_fm(p, p.g, wb, W["ab_wq"], D, D, wl0)
    p.t.wait(wl0, wl0.v)
    for och in range(NFC):
        for tg in range(4):
            emit_mm_fm(p, ps_s, wb, och, hn_fm, tg * 512, (tg + 1) * 512,
                       NFC, sem_mm=mm)
            p.s.wait(mm, mm.v)
            p.s.activation(fm_ap(q_fm, och, tg * 512, (tg + 1) * 512),
                           ps_s[:, :], AF.Copy).then_inc(act)
            p.t.wait(act, act.v)

    mk = p.sbuf(tag + "_mk", [128, 4 * 512], BF16)
    p.g.dma_start(out=mk.ap().rearrange("p (o j) -> p o j", o=4),
                  in_=W["ab_mask"].rearrange("o p j -> p o j")
                  ).then_inc(wl1, 16)

    acc = p.sbuf(tag + "_acc", [128, 4 * D], F32)
    o_tm = hn_fm
    pb = p.sbuf(tag + "_pb", [128, 512], BF16)
    ptfm = p.sbuf(tag + "_ptfm", [128, 512], BF16)
    kb2 = [p.sbuf(tag + "_kb0", [128, NFC * 512], BF16),
           p.sbuf(tag + "_kb1", [128, NFC * 512], BF16)]
    vb2 = [p.sbuf(tag + "_vb0", [128, 4 * D], BF16),
           p.sbuf(tag + "_vb1", [128, 4 * D], BF16)]
    mprev = p.sbuf(tag + "_mprev", [128, 4], F32)
    mcur = p.sbuf(tag + "_mcur", [128, 4], F32)
    lprev = p.sbuf(tag + "_lp", [128, 4], F32)
    lcur = p.sbuf(tag + "_lc", [128, 4], F32)
    srow = p.sbuf(tag + "_srow", [128, 4], F32)
    nm2 = p.sbuf(tag + "_nm2", [128, 4], F32)
    nm2k = p.sbuf(tag + "_nm2k", [128, 4], F32)
    alp = p.sbuf(tag + "_alp", [128, 4], F32)
    rsum = p.sbuf(tag + "_rs", [128, 4], F32)
    rcp = p.sbuf(tag + "_rcp", [128, 8], F32)

    p.v.wait(cc, cc.v)
    p.t.wait(wl1, wl1.v)
    nblk = 0
    for qslot in range(2):
        for b in range(B):
            qt0 = qslot * 8 + b * 4
            off = qslot * 1024 + b * 512
            p.v.memset(mprev[:, :], -3.0e30)
            p.v.memset(lprev[:, :], 0.0)
            p.v.memset(acc[:, :], 0.0).then_inc(dve)
            blocks = [("own", off, True, None)]
            for sk in range(NSTRIPE):
                kr = sk if sk < 8 else 15 - sk
                kslot = 0 if sk < 8 else 1
                blocks.append(("ag", (kr, kslot * 1024 + b * 512), False,
                               qslot * 16 + sk))
            for srcw, loc, diag, kidx in blocks:
                kb, vb = kb2[nblk % 2], vb2[nblk % 2]
                kl, vl = kld[nblk % 2], vld[nblk % 2]
                if nblk >= 2:
                    p.g.wait(mm, mm.v)
                if srcw == "own":
                    kap = k_in[0:D]
                    vap = v_in[loc:loc + 512]
                    koff = loc
                else:
                    kr, koff = loc[0], loc[1]
                    kap = k_all[kr * D:(kr + 1) * D]
                    vap = v_all[kr * T + koff:kr * T + koff + 512]
                p.g.dma_start(
                    out=kb.ap().rearrange("p (c j) -> p c j", c=NFC),
                    in_=kap.rearrange("(c p) t -> p c t",
                                      p=128)[:, :, koff:koff + 512]
                    ).then_inc(kl, 16)
                p.g.dma_start(
                    out=vb.ap().rearrange("p (j d) -> p j d", j=4),
                    in_=vap.rearrange("(j p) d -> p j d", p=128)
                    ).then_inc(vl, 16)
                p.t.wait(kl, kl.v)
                p.t.wait(vl, vl.v)
                for j in range(4):
                    tt = qt0 + j
                    p.t.wait(act, act.v)
                    r = None
                    for ch in range(NFC):
                        r = p.t.matmul(ps_s[:, :],
                                       fm_ap(q_fm, ch, tt * 128, (tt + 1) * 128),
                                       kb[:, ch * 512:(ch + 1) * 512],
                                       start=(ch == 0), stop=(ch == NFC - 1))
                    if diag:
                        r = p.t.matmul(ps_s[:, :], ident[:, :],
                                       mk[:, j * 512:(j + 1) * 512],
                                       start=False, stop=True,
                                       skip_group_check=True)
                    r.then_inc(mm)
                    p.v.wait(mm, mm.v)
                    p.v.reduce_max(srow[:, j:j + 1], ps_s[:, :],
                                   axis=AX.X).then_inc(dve)
                    p.s.wait(dve, dve.v)
                    p.s.copy(rcp[:, j:j + 1], srow[:, j:j + 1]).then_inc(act)
                    p.v.wait(act, act.v)
                    p.v.tensor_max(mcur[:, j:j + 1], mprev[:, j:j + 1],
                                   rcp[:, j:j + 1]).then_inc(ev)
                    p.s.wait(ev, ev.v)
                    p.s.mul(rcp[:, 4 + j:5 + j], mcur[:, j:j + 1],
                            -scl).then_inc(act)
                    p.v.wait(act, act.v)
                    r = p.v.tensor_copy(nm2[:, j:j + 1], rcp[:, 4 + j:5 + j])
                    if kidx is not None:
                        r = p.v.tensor_add(nm2k[:, j:j + 1],
                                           rcp[:, 4 + j:5 + j],
                                           killb[:, kidx:kidx + 1])
                    r.then_inc(dve)
                    pbias = nm2k if kidx is not None else nm2
                    p.s.wait(dve, dve.v)
                    p.s.activation(alp[:, j:j + 1], mprev[:, j:j + 1], AF.Exp,
                                   scale=scl, bias=nm2[:, j:j + 1])
                    p.s.activation(pb[:, :], ps_s[:, :], AF.Exp, scale=scl,
                                   bias=pbias[:, j:j + 1],
                                   accum_out=rsum[:, j:j + 1]).then_inc(act)
                    p.v.wait(act, act.v)
                    p.v.scalar_tensor_tensor(lcur[:, j:j + 1], lprev[:, j:j + 1],
                                             alp[:, j:j + 1], rsum[:, j:j + 1],
                                             ALU.mult, ALU.add).then_inc(dve)
                    p.t.wait(act, act.v)
                    p.t.wait(dve, dve.v)
                    r = None
                    for jj in range(4):
                        r = p.t.transpose(ptp[:, jj * 128:(jj + 1) * 128],
                                          pb[:, jj * 128:(jj + 1) * 128],
                                          ident[:, :])
                    r.then_inc(pe)
                    p.v.wait(pe, pe.v)
                    p.v.wait(mm, mm.v)
                    p.v.tensor_copy(ptfm[:, :], ptp[:, :]).then_inc(dve)
                    p.t.wait(dve, dve.v)
                    r = None
                    for jj in range(4):
                        for h in range(2):
                            r = p.t.matmul(
                                ps_av[:, h * 512:(h + 1) * 512],
                                ptfm[:, jj * 128:(jj + 1) * 128],
                                vb[:, jj * D + h * 512:jj * D + (h + 1) * 512],
                                start=(jj == 0), stop=(jj == 3))
                    r.then_inc(mm)
                    p.v.wait(mm, mm.v)
                    p.v.scalar_tensor_tensor(acc[:, j * D:(j + 1) * D],
                                             acc[:, j * D:(j + 1) * D],
                                             alp[:, j:j + 1], ps_av[:, :],
                                             ALU.mult, ALU.add).then_inc(dve)
                    p.t.wait(dve, dve.v)
                p.v.wait(ev, ev.v)
                p.v.tensor_copy(mprev[:, :], mcur[:, :])
                p.v.tensor_copy(lprev[:, :], lcur[:, :]).then_inc(dve)
                nblk += 1
            p.v.reciprocal(rcp[:, 0:4], lprev[:, 0:4]).then_inc(dve)
            p.s.wait(dve, dve.v)
            p.s.copy(rcp[:, 4:8], rcp[:, 0:4]).then_inc(act)
            p.v.wait(act, act.v)
            for j in range(4):
                tt = qt0 + j
                p.v.tensor_scalar_mul(o_tm[:, tt * D:(tt + 1) * D],
                                      acc[:, j * D:(j + 1) * D],
                                      rcp[:, 4 + j:5 + j]).then_inc(dve)

    p.attnb_dbg = {"killb": killb, "otm": o_tm, "acc": acc,
                   "lprev": lprev, "mprev": mprev, "rcp": rcp,
                   "qfm": q_fm, "mk": mk}
    p.g.wait(mm, mm.v)
    dma_w_fm(p, p.g, wb, W["ab_wo"], D, D, wl0)
    p.t.wait(wl0, wl0.v)
    p.q.wait(dve, dve.v)
    ofm = p.sbuf(tag + "_ofm", [128, D], BF16)
    for tt in range(NTT):
        p.t.wait(dve, dve.v)
        r = None
        for ch in range(NFC):
            r = p.t.transpose(ptA[:, ch * 128:(ch + 1) * 128],
                              o_tm[:, tt * D + ch * 128:tt * D + (ch + 1) * 128],
                              ident[:, :])
        r.then_inc(pe)
        p.v.wait(pe, pe.v)
        p.v.tensor_copy(ofm[:, :], ptA[:, :]).then_inc(dve)
        p.t.wait(dve, dve.v)
        r = None
        for h in range(2):
            for ch in range(NFC):
                r = p.t.matmul(ps_av[:, h * 512:(h + 1) * 512],
                               ofm[:, ch * 128:(ch + 1) * 128],
                               wb[:, ch * D + h * 512:ch * D + (h + 1) * 512],
                               start=(ch == 0), stop=(ch == NFC - 1))
        r.then_inc(mm)
        sl = xldS[tt % 2]
        if tt >= 2:
            p.q.wait(dve, dve.v - 1)
        p.q.dma_start(out=xt2[tt % 2][:, :],
                      in_=xin[tt * 128:(tt + 1) * 128, :]).then_inc(sl, 16)
        p.v.wait(sl, sl.v)
        p.v.wait(mm, mm.v)
        p.v.tensor_add(xt2[tt % 2][:, :], xt2[tt % 2][:, :],
                       ps_av[:, :]).then_inc(dve)
        p.t.wait(dve, dve.v)
        p.q.wait(dve, dve.v)
        p.q.dma_start(out=xout[tt * 128:(tt + 1) * 128, :],
                      in_=xt2[tt % 2][:, :]).then_inc(sto, 16)
    return sto.v


def build_attnb_nc(dbg=False):
    nc = mk_nc()
    p = Prog(nc)
    xin = nc.dram_tensor("xin", [T, D], F32, kind="ExternalInput").ap()
    xout = nc.dram_tensor("xout", [T, D], F32, kind="ExternalOutput").ap()
    sems = mk_sems(p)
    W = declare_attnb(nc)
    ident = load_ident(p, nc, sems)
    emit_attnb(p, nc, xin, xout, W, ident, sems)
    p.q.wait(sems["sto"])
    if dbg:
        dsem = p.sem("dbgsem")
        nd = 0
        for k, t in p.attnb_dbg.items():
            o = nc.dram_tensor("dbg_" + k, list(t.shape), t.dtype,
                               kind="ExternalOutput")
            p.q.dma_start(out=o.ap(), in_=t.ap()).then_inc(dsem, 16)
            nd += 16
        p.q.wait(dsem, nd)
    return p.finalize()




def prep_l1attna_host(inputs):
    W = {}
    W["la_wqkv"] = np.ascontiguousarray(
        inputs["L1_attn_a_Wqkv"].transpose(1, 0, 2).reshape(D, 3072))
    W["la_bqkv"] = np.ascontiguousarray(
        inputs["L1_attn_a_bqkv"].reshape(1, 3072))
    W["la_wpT"] = np.ascontiguousarray(inputs["L1_attn_a_Wp"].T)
    W["la_bp"] = np.ascontiguousarray(inputs["L1_attn_a_bp"].reshape(1, D))
    return W


def declare_l1attna(nc):
    W = {"la_wqkv": nc.dram_tensor("la_wqkv", [D, 3072], F32,
                                   kind="ExternalInput").ap(),
         "la_bqkv": nc.dram_tensor("la_bqkv", [1, 3072], F32,
                                   kind="ExternalInput").ap(),
         "la_wpT": nc.dram_tensor("la_wpT", [D, D], F32,
                                  kind="ExternalInput").ap(),
         "la_bp": nc.dram_tensor("la_bp", [1, D], F32,
                                 kind="ExternalInput").ap()}
    return W


def emit_l1attna(p, nc, xin, xout, W, ident, sems, tag="la"):
    xldS, dve, act, pe, ev, mm, sto = (
        sems[k] for k in ("xldS", "dve", "act", "pe", "ev", "mm", "sto"))
    wl0, wl1, wl2 = sems["wlS"]
    cc = p.sem(tag + "_cc")
    kvs = p.sem(tag + "_kvs")
    kld = [p.sem(tag + "_kld0"), p.sem(tag + "_kld1")]
    vld = [p.sem(tag + "_vld0"), p.sem(tag + "_vld1")]
    SCL = 8.0

    st = mk_stats(p, tag)
    hn_fm = p.sbuf(tag + "_hnfm", [128, NFC * T], BF16)
    xt2, rstd_ready = emit_stream_ln(p, xin, st, sems, tag)

    k_in = nc.dram_tensor(tag + "_kin", [D, T], BF16)
    v_in = nc.dram_tensor(tag + "_vin", [T, D], BF16)
    k_all = nc.dram_tensor(tag + "_kall", [NCORES * D, T], BF16)
    v_all = nc.dram_tensor(tag + "_vall", [NCORES * T, D], BF16)

    wb = p.sbuf(tag + "_wb", [128, NFC * D], BF16)
    q_fm = p.sbuf(tag + "_qfm", [128, NFC * T], BF16)
    k_fm = p.sbuf(tag + "_kfm", [128, NFC * T], BF16)
    vtile = p.sbuf(tag + "_vt", [128, 2 * D], BF16)
    bq = p.sbuf(tag + "_bq", [1, 3072], BF16)
    ones512 = p.sbuf(tag + "_o512", [1, 512], BF16)
    ps_s = p.psum(tag + "_pss", [128, 4 * 512])
    ps_av = p.psum(tag + "_psav", [128, 1024])
    ptp = p.psum(tag + "_ptp", [128, 4 * 512], BF16)
    ptA = ptp[:, 0:D]
    ones1 = p.sbuf(tag + "_ones1", [1, 128], BF16)
    emit_stream_norm_transpose(p, xin, st, hn_fm, ident, xt2,
                               [ptp[:, 0:D], ptp[:, D:2 * D]], sems,
                               rstd_ready)

    p.v.memset(ones1[:, :], 1.0)
    p.v.memset(ones512[:, :], 1.0).then_inc(dve)
    p.g.dma_start(out=bq[:, :], in_=W["la_bqkv"]).then_inc(wl2, 16)
    p.t.wait(wl2, wl2.v)
    p.t.wait(dve, dve.v)

    def qkv_fm(dst_fm, col0):
        for och in range(NFC):
            oc_glob = col0 + och * 128
            for tg in range(4):
                emit_mm_fm(p, ps_s[:, 0:512], wb, och, hn_fm, tg * 512,
                           (tg + 1) * 512, NFC)
                p.t.matmul(ps_s[:, 0:512],
                           bq[0:1, oc_glob:oc_glob + 128],
                           ones512[:, :], start=False, stop=True,
                           skip_group_check=True).then_inc(mm)
                p.s.wait(mm, mm.v)
                p.s.activation(fm_ap(dst_fm, och, tg * 512, (tg + 1) * 512),
                               ps_s[:, 0:512], AF.Copy).then_inc(act)
                p.t.wait(act, act.v)

    dma_w_fm(p, p.g, wb, W["la_wqkv"], D, 3072, wl0, col0=1024, ncols=1024)
    p.t.wait(wl0, wl0.v)
    qkv_fm(k_fm, 1024)
    p.g.wait(act, act.v)
    p.g.dma_start(out=k_in.ap().rearrange("(c p) t -> p c t", p=128),
                  in_=k_fm.ap().rearrange("p (c t) -> p c t", c=NFC)
                  ).then_inc(kvs, 16)
    p.g.wait(mm, mm.v)
    dma_w_fm(p, p.g, wb, W["la_wqkv"], D, 3072, wl0, col0=2048, ncols=1024)
    p.t.wait(wl0, wl0.v)
    kvs0 = kvs.v
    for tt in range(NTT):
        emit_mm_tm(p, ps_av, hn_fm, wb, tt, NFC, 0, D, sem_mm=None, acc=False)
        for h2 in range(2):
            p.t.matmul(ps_av[:, h2 * 512:(h2 + 1) * 512], ones1[:, :],
                       bq[0:1, 2048 + h2 * 512:2048 + (h2 + 1) * 512],
                       start=False, stop=True,
                       skip_group_check=True)
        p.events[-1][1].then_inc(mm)
        p.s.wait(mm, mm.v)
        if tt >= 2:
            p.s.wait(kvs, kvs0 + 16 * (tt - 1))
        p.s.activation(vtile[:, (tt % 2) * D:(tt % 2 + 1) * D], ps_av[:, :],
                       AF.Copy).then_inc(act)
        p.g.wait(act, act.v)
        p.g.dma_start(out=v_in[tt * 128:(tt + 1) * 128, :],
                      in_=vtile[:, (tt % 2) * D:(tt % 2 + 1) * D]
                      ).then_inc(kvs, 16)
        p.t.wait(act, act.v)
    p.g.wait(kvs, kvs.v)
    p.g.collective_compute(
        "AllGather", ALU.bypass, replica_groups=[list(range(NCORES))],
        ins=[k_in.ap().opt()], outs=[k_all.ap().opt()]).inc_default(cc)
    p.g.collective_compute(
        "AllGather", ALU.bypass, replica_groups=[list(range(NCORES))],
        ins=[v_in.ap().opt()], outs=[v_all.ap().opt()]).inc_default(cc)
    p.g.wait(cc, cc.v)
    p.g.wait(mm, mm.v)
    dma_w_fm(p, p.g, wb, W["la_wqkv"], D, 3072, wl0, col0=0, ncols=1024)
    p.t.wait(wl0, wl0.v)
    qkv_fm(q_fm, 0)

    acc = p.sbuf(tag + "_acc", [128, 4 * 1024], F32)
    o_tm = hn_fm
    pb = p.sbuf(tag + "_pb", [128, 4 * 512], BF16)
    ptfm = p.sbuf(tag + "_ptfm", [128, 4 * 512], BF16)
    kb2 = [p.sbuf(tag + "_kb0", [128, NFC * 512], BF16),
           p.sbuf(tag + "_kb1", [128, NFC * 512], BF16)]
    vb2 = [p.sbuf(tag + "_vb0", [128, 4 * D], BF16),
           p.sbuf(tag + "_vb1", [128, 4 * D], BF16)]
    mA = p.sbuf(tag + "_mA", [128, 64], F32)
    mC = p.sbuf(tag + "_mC", [128, 64], F32)
    lA = [p.sbuf(tag + "_lA0", [128, 64], F32),
          p.sbuf(tag + "_lA1", [128, 64], F32)]
    srow = p.sbuf(tag + "_srow", [128, 4], F32)
    srowc = p.sbuf(tag + "_srowc", [128, 4], F32)
    nm2a = p.sbuf(tag + "_nm2a", [128, 4], F32)
    nm2 = p.sbuf(tag + "_nm2", [128, 4], F32)
    ddm = p.sbuf(tag + "_ddm", [128, 4], F32)
    alp = p.sbuf(tag + "_alp", [128, 4], F32)
    rsum = p.sbuf(tag + "_rs", [128, 4], F32)
    rcp = p.sbuf(tag + "_rcp", [128, 64], F32)

    p.v.wait(cc, cc.v)
    nblk = 0
    for qslot in range(2):
        for b in range(B):
            qt0 = qslot * 8 + b * 4
            p.v.wait(act, act.v)
            p.v.memset(mA[:, :], -3.0e30)
            p.v.memset(lA[0][:, :], 0.0)
            p.v.memset(acc[:, :], 0.0).then_inc(dve)
            p.s.wait(dve, dve.v)
            p.s.copy(mC[:, :], mA[:, :]).then_inc(act)
            lpi = 0
            for sk in range(NSTRIPE):
                kr = sk if sk < 8 else 15 - sk
                koff = (0 if sk < 8 else 1024) + b * 512
                kb, vb = kb2[nblk % 2], vb2[nblk % 2]
                kl, vl = kld[nblk % 2], vld[nblk % 2]
                if nblk >= 2:
                    p.g.wait(mm, mm.v)
                p.g.dma_start(
                    out=kb.ap().rearrange("p (c j) -> p c j", c=NFC),
                    in_=k_all[kr * D:(kr + 1) * D].rearrange(
                        "(c p) t -> p c t", p=128)[:, :, koff:koff + 512]
                    ).then_inc(kl, 16)
                vbase = kr * T + koff
                p.g.dma_start(
                    out=vb.ap().rearrange("p (j d) -> p j d", j=4),
                    in_=v_all[vbase:vbase + 512].rearrange(
                        "(j p) d -> p j d", p=128)).then_inc(vl, 16)
                p.t.wait(kl, kl.v)
                p.t.wait(vl, vl.v)
                for j in range(4):
                    tt = qt0 + j
                    for hg in range(4):
                        mcol = j * 16 + hg * 4
                        p.t.wait(act, act.v)
                        r = None
                        for hl in range(4):
                            h = hg * 4 + hl
                            ch = h // 2
                            hb = (h % 2) * 64
                            r = p.t.matmul(
                                ps_s[:, hl * 512:(hl + 1) * 512],
                                q_fm[hb:hb + 64,
                                     ch * T + tt * 128:ch * T + (tt + 1) * 128],
                                kb[hb:hb + 64, ch * 512:(ch + 1) * 512],
                                start=True, stop=True)
                        r.then_inc(mm)
                        p.v.wait(mm, mm.v)
                        p.v.reduce_max(
                            srow[:, 0:4],
                            ps_s.ap().rearrange("p (h k) -> p h k", h=4),
                            axis=AX.X).then_inc(dve)
                        p.s.wait(dve, dve.v)
                        p.s.copy(srowc[:, :], srow[:, 0:4]).then_inc(act)
                        p.v.wait(act, act.v)
                        p.v.tensor_max(mA[:, mcol:mcol + 4],
                                       mC[:, mcol:mcol + 4],
                                       srowc[:, :]).then_inc(dve)
                        p.s.wait(dve, dve.v)
                        p.s.mul(nm2a[:, :], mA[:, mcol:mcol + 4],
                                -SCL).then_inc(act)
                        p.v.wait(act, act.v)
                        p.v.tensor_copy(nm2[:, :], nm2a[:, :])
                        p.v.scalar_tensor_tensor(
                            ddm[:, :], nm2a[:, :], 1.0 / SCL,
                            mC[:, mcol:mcol + 4], ALU.mult,
                            ALU.add).then_inc(dve)
                        p.s.wait(dve, dve.v)
                        p.s.activation(alp[:, :], ddm[:, :], AF.Exp,
                                       scale=SCL)
                        r = None
                        for hl in range(4):
                            r = p.s.activation(
                                pb[:, hl * 512:(hl + 1) * 512],
                                ps_s[:, hl * 512:(hl + 1) * 512], AF.Exp,
                                scale=SCL, bias=nm2[:, hl:hl + 1],
                                accum_out=rsum[:, hl:hl + 1])
                        r.then_inc(act)
                        p.s.copy(mC[:, mcol:mcol + 4],
                                 mA[:, mcol:mcol + 4]).then_inc(act)
                        lp, lc = lA[lpi], lA[1 - lpi]
                        p.v.wait(act, act.v)
                        for hl in range(4):
                            p.v.scalar_tensor_tensor(
                                lc[:, mcol + hl:mcol + hl + 1],
                                lp[:, mcol + hl:mcol + hl + 1],
                                alp[:, hl:hl + 1], rsum[:, hl:hl + 1],
                                ALU.mult, ALU.add)
                        p.events[-1][1].then_inc(dve)
                        p.t.wait(act, act.v)
                        p.t.wait(dve, dve.v)
                        r = None
                        for jj in range(4):
                            for hl in range(4):
                                r = p.t.transpose(
                                    ptp[:, hl * 512 + jj * 128:
                                        hl * 512 + (jj + 1) * 128],
                                    pb[:, hl * 512 + jj * 128:
                                       hl * 512 + (jj + 1) * 128],
                                    ident[:, :])
                        r.then_inc(pe)
                        p.v.wait(pe, pe.v)
                        p.v.tensor_copy(ptfm[:, :], ptp[:, :]).then_inc(dve)
                        p.t.wait(dve, dve.v)
                        r = None
                        for hl in range(4):
                            h = hg * 4 + hl
                            for jj in range(4):
                                r = p.t.matmul(
                                    ps_av[:, hl * 64:(hl + 1) * 64],
                                    ptfm[:, hl * 512 + jj * 128:
                                         hl * 512 + (jj + 1) * 128],
                                    vb[:, jj * D + h * 64:jj * D + (h + 1) * 64],
                                    start=(jj == 0), stop=(jj == 3))
                        r.then_inc(mm)
                        p.v.wait(mm, mm.v)
                        for hl in range(4):
                            h = hg * 4 + hl
                            p.v.scalar_tensor_tensor(
                                acc[:, j * 1024 + h * 64:j * 1024 + (h + 1) * 64],
                                acc[:, j * 1024 + h * 64:j * 1024 + (h + 1) * 64],
                                alp[:, hl:hl + 1],
                                ps_av[:, hl * 64:(hl + 1) * 64],
                                ALU.mult, ALU.add)
                        p.events[-1][1].then_inc(dve)
                        p.t.wait(dve, dve.v)
                lpi = 1 - lpi
                nblk += 1
            p.v.reciprocal(rcp[:, 0:64],
                           lA[lpi][:, 0:64]).then_inc(dve)
            p.s.wait(dve, dve.v)
            p.s.copy(rcp[:, 0:64], rcp[:, 0:64]).then_inc(act)
            p.v.wait(act, act.v)
            for j in range(4):
                tt = qt0 + j
                for h in range(16):
                    p.v.tensor_scalar_mul(
                        o_tm[:, tt * D + h * 64:tt * D + (h + 1) * 64],
                        acc[:, j * 1024 + h * 64:j * 1024 + (h + 1) * 64],
                        rcp[:, j * 16 + h:j * 16 + h + 1])
            p.events[-1][1].then_inc(dve)

    p.g.wait(mm, mm.v)
    dma_w_fm(p, p.g, wb, W["la_wpT"], D, D, wl0)
    p.g.dma_start(out=bq[0:1, 0:D], in_=W["la_bp"]).then_inc(wl2, 16)
    p.t.wait(wl0, wl0.v)
    p.t.wait(wl2, wl2.v)
    p.q.wait(dve, dve.v)
    ofm = p.sbuf(tag + "_ofm", [128, D], BF16)
    for tt in range(NTT):
        p.t.wait(dve, dve.v)
        r = None
        for ch in range(NFC):
            r = p.t.transpose(ptA[:, ch * 128:(ch + 1) * 128],
                              o_tm[:, tt * D + ch * 128:tt * D + (ch + 1) * 128],
                              ident[:, :])
        r.then_inc(pe)
        p.v.wait(pe, pe.v)
        p.v.tensor_copy(ofm[:, :], ptA[:, :]).then_inc(dve)
        p.t.wait(dve, dve.v)
        r = None
        for h2 in range(2):
            for ch in range(NFC):
                r = p.t.matmul(ps_av[:, h2 * 512:(h2 + 1) * 512],
                               ofm[:, ch * 128:(ch + 1) * 128],
                               wb[:, ch * D + h2 * 512:ch * D + (h2 + 1) * 512],
                               start=(ch == 0), stop=False,
                               skip_group_check=True)
            r = p.t.matmul(ps_av[:, h2 * 512:(h2 + 1) * 512], ones1[:, :],
                           bq[0:1, h2 * 512:(h2 + 1) * 512], start=False,
                           stop=True, skip_group_check=True)
        r.then_inc(mm)
        sl = xldS[tt % 2]
        if tt >= 2:
            p.q.wait(dve, dve.v - 1)
        p.q.dma_start(out=xt2[tt % 2][:, :],
                      in_=xin[tt * 128:(tt + 1) * 128, :]).then_inc(sl, 16)
        p.v.wait(sl, sl.v)
        p.v.wait(mm, mm.v)
        p.v.tensor_add(xt2[tt % 2][:, :], xt2[tt % 2][:, :],
                       ps_av[:, :]).then_inc(dve)
        p.t.wait(dve, dve.v)
        p.q.wait(dve, dve.v)
        p.q.dma_start(out=xout[tt * 128:(tt + 1) * 128, :],
                      in_=xt2[tt % 2][:, :]).then_inc(sto, 16)
    return sto.v


def build_l1attna_nc():
    nc = mk_nc()
    p = Prog(nc)
    xin = nc.dram_tensor("xin", [T, D], F32, kind="ExternalInput").ap()
    xout = nc.dram_tensor("xout", [T, D], F32, kind="ExternalOutput").ap()
    sems = mk_sems(p)
    W = declare_l1attna(nc)
    ident = load_ident(p, nc, sems)
    emit_l1attna(p, nc, xin, xout, W, ident, sems)
    p.q.wait(sems["sto"])
    return p.finalize()




def prep_l1attnb_host(inputs):
    return {"lb_wqkvT": np.ascontiguousarray(inputs["L1_attn_b_in_proj_weight"].T),
            "lb_woT": np.ascontiguousarray(inputs["L1_attn_b_out_weight"].T)}


def declare_l1attnb(nc):
    return {"lb_wqkvT": nc.dram_tensor("lb_wqkvT", [D, 3072], F32,
                                       kind="ExternalInput").ap(),
            "lb_woT": nc.dram_tensor("lb_woT", [D, D], F32,
                                     kind="ExternalInput").ap()}


def emit_l1attnb(p, nc, xin, xout, W, ident, sems, tag="lb"):
    xldS, dve, act, pe, ev, mm, sto = (
        sems[k] for k in ("xldS", "dve", "act", "pe", "ev", "mm", "sto"))
    wl0, wl1, wl2 = sems["wlS"]
    SC = 0.125

    st = mk_stats(p, tag)
    hn_fm = p.sbuf(tag + "_hnfm", [128, NFC * T], BF16)
    xt2, rstd_ready = emit_stream_ln(p, xin, st, sems, tag)
    ptp = p.psum(tag + "_ptp", [128, 2 * D], BF16)
    emit_stream_norm_transpose(p, xin, st, hn_fm, ident, xt2,
                               [ptp[:, 0:D], ptp[:, D:2 * D]], sems,
                               rstd_ready)

    wqkv = p.sbuf(tag + "_wqkv", [128, NFC * 3072], BF16)
    dma_w_fm(p, p.g, wqkv, W["lb_wqkvT"], D, 3072, wl0)
    o_tm = p.sbuf(tag + "_otm", [128, NTT * D], BF16)
    qkvA = p.sbuf(tag + "_qkvA", [128, 3072], BF16)
    qkvB = p.sbuf(tag + "_qkvB", [128, 3072], BF16)
    sc4 = p.sbuf(tag + "_sc4", [128, 64], F32)
    sc4c = p.sbuf(tag + "_sc4c", [128, 64], F32)
    dA = p.sbuf(tag + "_dA", [128, 16], F32)
    dB = p.sbuf(tag + "_dB", [128, 16], F32)
    pA = p.sbuf(tag + "_pA", [128, 16], BF16)
    pB = p.sbuf(tag + "_pB", [128, 16], BF16)
    dv = p.sbuf(tag + "_dv", [128, 1024], BF16)
    vAc = p.sbuf(tag + "_vAc", [128, 1024], BF16)
    vBc = p.sbuf(tag + "_vBc", [128, 1024], BF16)
    tmpm = p.sbuf(tag + "_tmpm", [128, 1024], F32)
    pT = p.sbuf(tag + "_pT", [16, 128], BF16)
    esel = p.sbuf(tag + "_esel", [16, 1024], BF16)
    selin = nc.dram_tensor(tag + "_selin", [16, 1024], F32,
                           kind="ExternalInput").ap()
    p.g.dma_start(out=esel[:, :], in_=selin).then_inc(wl2, 16)
    ps1 = p.psum(tag + "_ps1", [128, 1024])
    ps2 = p.psum(tag + "_ps2", [128, 1024])

    p.t.wait(wl0, wl0.v)
    p.t.wait(wl2, wl2.v)
    hview = lambda buf, lo, hi: buf.ap().rearrange(
        "p (h z) -> p h z", h=16)[:, :, lo:hi]
    for slot in range(2):
        for pt in range(4):
            ttA = slot * 8 + pt
            ttB = slot * 8 + 4 + pt
            for (tt, dst) in ((ttA, qkvA), (ttB, qkvB)):
                for j6 in range(6):
                    p.t.wait(act, act.v)
                    emit_mm_tm(p, ps1[:, 0:512], hn_fm, wqkv, tt, NFC,
                               j6 * 512, 512, sem_mm=mm)
                    p.s.wait(mm, mm.v)
                    p.s.activation(dst[:, j6 * 512:(j6 + 1) * 512],
                                   ps1[:, 0:512], AF.Copy).then_inc(act)
            qA, kA, vA = (hview(qkvA, 0, 64), hview(qkvA, 64, 128),
                          hview(qkvA, 128, 192))
            qB, kB, vB = (hview(qkvB, 0, 64), hview(qkvB, 64, 128),
                          hview(qkvB, 128, 192))
            tm3 = tmpm.ap().rearrange("p (h z) -> p h z", h=16)[:, :, 0:64]
            p.v.wait(act, act.v)
            for i, (qq, kk) in enumerate(((qA, kA), (qA, kB),
                                          (qB, kA), (qB, kB))):
                p.v.tensor_mul(tm3, qq, kk)
                p.v.reduce_sum(sc4[:, i * 16:(i + 1) * 16],
                               tmpm.ap().rearrange("p (h z) -> p h z", h=16),
                               axis=AX.X)
            p.v.tensor_copy(vAc[:, :].rearrange("p (h z) -> p h z", h=16),
                            vA)
            p.v.tensor_copy(vBc[:, :].rearrange("p (h z) -> p h z", h=16),
                            vB).then_inc(dve)
            p.s.wait(dve, dve.v)
            p.s.copy(sc4c[:, :], sc4[:, :]).then_inc(act)
            p.v.wait(act, act.v)
            p.v.tensor_sub(dA[:, :], sc4c[:, 0:16], sc4c[:, 16:32])
            p.v.tensor_sub(dB[:, :], sc4c[:, 32:48], sc4c[:, 48:64])
            p.v.tensor_sub(dv[:, :], vAc[:, :], vBc[:, :]).then_inc(dve)
            p.s.wait(dve, dve.v)
            p.s.activation(pA[:, :], dA[:, :], AF.Sigmoid, scale=SC)
            p.s.activation(pB[:, :], dB[:, :], AF.Sigmoid,
                           scale=SC).then_inc(act)
            for (pp, tt, pso) in ((pA, ttA, ps1), (pB, ttB, ps2)):
                p.t.wait(act, act.v)
                p.t.wait(dve, dve.v)
                p.t.transpose(ptp[0:16, 0:128], pp[:, :],
                              ident[:, :]).then_inc(pe)
                p.v.wait(pe, pe.v)
                p.v.tensor_copy(pT[:, :], ptp[0:16, 0:128]).then_inc(dve)
                p.t.wait(dve, dve.v)
                r = None
                for h2 in range(2):
                    r = p.t.matmul(pso[:, h2 * 512:(h2 + 1) * 512],
                                   pT[:, :],
                                   esel[:, h2 * 512:(h2 + 1) * 512],
                                   start=True, stop=True)
                r.then_inc(mm)
                p.v.wait(mm, mm.v)
                p.v.tensor_mul(tmpm[:, :], dv[:, :], pso[:, :])
                p.v.tensor_add(o_tm[:, tt * D:(tt + 1) * D], vBc[:, :],
                               tmpm[:, :]).then_inc(dve)
                p.t.wait(dve, dve.v)

    wb = p.sbuf(tag + "_wb", [128, NFC * D], BF16)
    p.g.wait(mm, mm.v)
    dma_w_fm(p, p.g, wb, W["lb_woT"], D, D, wl0)
    p.t.wait(wl0, wl0.v)
    p.q.wait(dve, dve.v)
    ofm = p.sbuf(tag + "_ofm", [128, D], BF16)
    for tt in range(NTT):
        p.t.wait(dve, dve.v)
        r = None
        for ch in range(NFC):
            r = p.t.transpose(ptp[:, ch * 128:(ch + 1) * 128],
                              o_tm[:, tt * D + ch * 128:tt * D + (ch + 1) * 128],
                              ident[:, :])
        r.then_inc(pe)
        p.v.wait(pe, pe.v)
        p.v.tensor_copy(ofm[:, :], ptp[:, 0:D]).then_inc(dve)
        p.t.wait(dve, dve.v)
        r = None
        for h2 in range(2):
            for ch in range(NFC):
                r = p.t.matmul(ps1[:, h2 * 512:(h2 + 1) * 512],
                               ofm[:, ch * 128:(ch + 1) * 128],
                               wb[:, ch * D + h2 * 512:ch * D + (h2 + 1) * 512],
                               start=(ch == 0), stop=(ch == NFC - 1))
        r.then_inc(mm)
        sl = xldS[tt % 2]
        if tt >= 2:
            p.q.wait(dve, dve.v - 1)
        p.q.dma_start(out=xt2[tt % 2][:, :],
                      in_=xin[tt * 128:(tt + 1) * 128, :]).then_inc(sl, 16)
        p.v.wait(sl, sl.v)
        p.v.wait(mm, mm.v)
        p.v.tensor_add(xt2[tt % 2][:, :], xt2[tt % 2][:, :],
                       ps1[:, :]).then_inc(dve)
        p.t.wait(dve, dve.v)
        p.q.wait(dve, dve.v)
        p.q.dma_start(out=xout[tt * 128:(tt + 1) * 128, :],
                      in_=xt2[tt % 2][:, :]).then_inc(sto, 16)
    return sto.v


def prep_l1attnb_sel():
    e = np.zeros((16, 1024), np.float32)
    for h in range(16):
        e[h, h * 64:(h + 1) * 64] = 1.0
    return e


def build_l1attnb_nc():
    nc = mk_nc()
    p = Prog(nc)
    xin = nc.dram_tensor("xin", [T, D], F32, kind="ExternalInput").ap()
    xout = nc.dram_tensor("xout", [T, D], F32, kind="ExternalOutput").ap()
    sems = mk_sems(p)
    W = declare_l1attnb(nc)
    ident = load_ident(p, nc, sems)
    emit_l1attnb(p, nc, xin, xout, W, ident, sems)
    p.q.wait(sems["sto"])
    return p.finalize()



RMS_EPS = float(np.finfo(np.float32).eps)


def prep_l0attn_host(inputs, core):
    W = {"aa_cqT": np.ascontiguousarray(inputs["L0_attn_a_cq_w"].T),
         "aa_ckT": np.ascontiguousarray(inputs["L0_attn_a_ck_w"].T),
         "aa_cvT": np.ascontiguousarray(inputs["L0_attn_a_cv_w"].T),
         "aa_cprojT": np.ascontiguousarray(inputs["L0_attn_a_cproj_w"].T),
         "aa_fcT": np.ascontiguousarray(inputs["L0_attn_a_fc_w"].T),
         "aa_projT": np.ascontiguousarray(inputs["L0_attn_a_proj_w"].T),
         "aa_ln1g": np.ascontiguousarray(
             inputs["L0_attn_a_ln1_w"].reshape(8, 128).T),
         "aa_ln1b": np.ascontiguousarray(
             inputs["L0_attn_a_ln1_b"].reshape(8, 128).T),
         "aa_ln2g": np.ascontiguousarray(
             inputs["L0_attn_a_ln2_w"].reshape(8, 128).T),
         "aa_ln2b": np.ascontiguousarray(
             inputs["L0_attn_a_ln2_b"].reshape(8, 128).T),
         "aa_gain": np.ascontiguousarray(
             inputs["L0_attn_a_q_gain"].reshape(1, 16))}
    pos = global_positions(core).astype(np.float64)
    inv = 1.0 / (10000.0 ** (np.arange(0, 16, 2) / 16.0))
    f = np.outer(pos, inv)
    cos = np.cos(f).astype(np.float32)
    sin = np.sin(f).astype(np.float32)
    W["aa_cos"] = np.ascontiguousarray(np.tile(cos, (1, 16)))
    W["aa_sin"] = np.ascontiguousarray(np.tile(sin, (1, 16)))
    m = np.zeros((4, 128, 512), np.float32)
    for o in range(4):
        q = np.arange(128)[:, None] + o * 128
        k = np.arange(512)[None, :]
        m[o] = np.where(k <= q, 0.0, MINF)
    W["aa_mask"] = m
    W["aa_kill"] = prep_attnb_kill(core)
    return W


def declare_l0attn(nc):
    W = {}
    for nm, shp in (("aa_cqT", [D, D]), ("aa_ckT", [D, 256]),
                    ("aa_cvT", [D, 256]), ("aa_cprojT", [D, D]),
                    ("aa_fcT", [D, 3072]), ("aa_projT", [3072, D])):
        W[nm] = nc.dram_tensor(nm, shp, F32, kind="ExternalInput").ap()
    for nm in ("aa_ln1g", "aa_ln1b", "aa_ln2g", "aa_ln2b"):
        W[nm] = nc.dram_tensor(nm, [128, 8], F32, kind="ExternalInput").ap()
    W["aa_gain"] = nc.dram_tensor("aa_gain", [1, 16], F32,
                                  kind="ExternalInput").ap()
    W["aa_cos"] = nc.dram_tensor("aa_cos", [T, 128], F32,
                                 kind="ExternalInput").ap()
    W["aa_sin"] = nc.dram_tensor("aa_sin", [T, 128], F32,
                                 kind="ExternalInput").ap()
    W["aa_mask"] = nc.dram_tensor("aa_mask", [4, 128, 512], F32,
                                  kind="ExternalInput").ap()
    W["aa_kill"] = nc.dram_tensor("aa_kill", [1, 32], F32,
                                  kind="ExternalInput").ap()
    return W


def emit_rms_rope(p, nc, W, q_tm, nheads, ntiles, sems, gain_bc, tag,
                  cos_sb, sin_sb, sq_scr, ssq, rstd):
    dve, act = sems["dve"], sems["act"]
    dh = nheads * 64
    r1 = p.sbuf(tag + "_r1", [128, 128], F32)
    r2 = p.sbuf(tag + "_r2", [128, 128], F32)
    r3 = p.sbuf(tag + "_r3", [128, 128], F32)
    r4 = p.sbuf(tag + "_r4", [128, 128], F32)
    for tt in range(ntiles):
        tile = q_tm[:, tt * dh:(tt + 1) * dh]
        t3 = q_tm.ap().rearrange("p (n h z) -> p n h z", n=ntiles,
                                 h=nheads)[:, tt, :, :]
        p.s.wait(dve, dve.v)
        p.s.activation(sq_scr[:, 0:dh], tile, AF.Square).then_inc(act)
        p.v.wait(act, act.v)
        p.v.reduce_sum(ssq[:, 0:nheads],
                       sq_scr[:, 0:dh].rearrange("p (h z) -> p h z",
                                                 h=nheads),
                       axis=AX.X).then_inc(dve)
        p.s.wait(dve, dve.v)
        p.s.activation(rstd[:, 0:nheads], ssq[:, 0:nheads], AF.Copy,
                       scale=1.0 / 64.0, bias=RMS_EPS)
        p.s.activation(rstd[:, 0:nheads], rstd[:, 0:nheads], AF.Ln)
        p.s.activation(rstd[:, 0:nheads], rstd[:, 0:nheads], AF.Exp,
                       scale=-0.5).then_inc(act)
        if gain_bc is not None:
            p.v.wait(act, act.v)
            p.v.tensor_mul(rstd[:, nheads:2 * nheads], rstd[:, 0:nheads],
                           gain_bc[:, 0:nheads]).then_inc(dve)
            p.s.wait(dve, dve.v)
            p.s.copy(rstd[:, 0:nheads], rstd[:, nheads:2 * nheads]
                     ).then_inc(act)
        p.v.wait(act, act.v)
        for h in range(nheads):
            p.v.tensor_scalar_mul(tile[:, h * 64:(h + 1) * 64],
                                  tile[:, h * 64:(h + 1) * 64],
                                  rstd[:, h:h + 1])
        p.events[-1][1].then_inc(dve)
        q1 = t3[:, :, 0:8]
        q2 = t3[:, :, 8:16]
        c8 = cos_sb.ap().rearrange("p (n z) -> p n z", n=ntiles)[
            :, tt, 0:nheads * 8].rearrange("p (h z) -> p h z", h=nheads)
        s8 = sin_sb.ap().rearrange("p (n z) -> p n z", n=ntiles)[
            :, tt, 0:nheads * 8].rearrange("p (h z) -> p h z", h=nheads)
        r1v = r1.ap().rearrange("p (h z) -> p h z", h=16)[:, 0:nheads, 0:8]
        r2v = r2.ap().rearrange("p (h z) -> p h z", h=16)[:, 0:nheads, 0:8]
        r3v = r3.ap().rearrange("p (h z) -> p h z", h=16)[:, 0:nheads, 0:8]
        r4v = r4.ap().rearrange("p (h z) -> p h z", h=16)[:, 0:nheads, 0:8]
        p.v.tensor_mul(r1v, q1, c8)
        p.v.tensor_mul(r3v, q1, s8)
        p.v.tensor_mul(r2v, q2, s8)
        p.v.tensor_mul(r4v, q2, c8)
        p.v.tensor_add(q1, r1v, r2v)
        p.v.tensor_sub(q2, r4v, r3v)
        p.events[-1][1].then_inc(dve)


def emit_l0attn(p, nc, xin, uin, xout, W, ident, sems, tag="aa", dbg=False):
    xldS, dve, act, pe, ev, mm, sto = (
        sems[k] for k in ("xldS", "dve", "act", "pe", "ev", "mm", "sto"))
    wl0, wl1, wl2 = sems["wlS"]
    cc = p.sem(tag + "_cc")
    kvs = p.sem(tag + "_kvs")
    kld = [p.sem(tag + "_kld0"), p.sem(tag + "_kld1")]
    vld = [p.sem(tag + "_vld0"), p.sem(tag + "_vld1")]
    vstore = p.sem(tag + "_vst")
    SCL = 0.125

    if dbg:
        vmid = nc.dram_tensor(tag + "_vmid", [T, D], F32,
                              kind="ExternalOutput")
    else:
        vmid = nc.dram_tensor(tag + "_vmid", [T, D], F32)
    k_in = nc.dram_tensor(tag + "_kin", [256, T], BF16)
    v_in = nc.dram_tensor(tag + "_vin", [T, 256], BF16)
    k_all = nc.dram_tensor(tag + "_kall", [NCORES * 256, T], BF16)
    v_all = nc.dram_tensor(tag + "_vall", [NCORES * T, 256], BF16)

    g1 = p.sbuf(tag + "_g1", [128, 8], F32)
    b1 = p.sbuf(tag + "_b1", [128, 8], F32)
    g2 = p.sbuf(tag + "_g2", [128, 8], F32)
    b2 = p.sbuf(tag + "_b2", [128, 8], F32)
    for nm, t_ in (("aa_ln1g", g1), ("aa_ln1b", b1), ("aa_ln2g", g2),
                   ("aa_ln2b", b2)):
        p.q.dma_start(out=t_[:, :], in_=W[nm]).then_inc(sems["sld"], 16)
    p.v.wait(sems["sld"])

    st = mk_stats(p, tag)
    hn_fm = p.sbuf(tag + "_hnfm", [128, NFC * T], BF16)
    xt2, rstd_ready = emit_stream_ln(p, uin, st, sems, tag)
    ptp = p.psum(tag + "_ptp", [128, D], BF16)
    ptq = p.psum(tag + "_ptq", [128, D], BF16)
    emit_stream_norm_transpose(p, uin, st, hn_fm, ident, xt2,
                               [ptp[:, 0:D], ptq[:, 0:D]], sems,
                               rstd_ready, gamma=g1, beta=b1)

    wb = p.sbuf(tag + "_wb", [128, 12 * 1024], BF16)
    q_tm = p.sbuf(tag + "_qtm", [128, NTT * D], BF16)
    q_fm = p.sbuf(tag + "_qfm", [128, NFC * T], BF16)
    k_tm = p.sbuf(tag + "_ktm", [128, NTT * 256], BF16)
    cos_sb = p.sbuf(tag + "_cosb", [128, NTT * 128], BF16)
    sin_sb = p.sbuf(tag + "_sinb", [128, NTT * 128], BF16)
    sq_scr = p.sbuf(tag + "_sqs", [128, D], F32)
    ssq = p.sbuf(tag + "_hssq", [128, 16], F32)
    rstd = p.sbuf(tag + "_hrstd", [128, 32], F32)
    gain_bc = p.sbuf(tag + "_gainb", [128, 16], F32)
    gr = p.sbuf(tag + "_gr", [1, 16], BF16)
    ones1 = p.sbuf(tag + "_ones1", [1, 128], BF16)
    killr = p.sbuf(tag + "_killr", [1, 32], BF16)
    killb = p.sbuf(tag + "_killb", [128, 32], F32)
    killbc = p.sbuf(tag + "_killbc", [128, 32], F32)
    ps_s = p.psum(tag + "_pss", [128, 4 * 512])
    ps_av = p.psum(tag + "_psav", [128, 1024])

    p.v.memset(ones1[:, :], 1.0).then_inc(dve)
    p.g.dma_start(out=gr[:, :], in_=W["aa_gain"]).then_inc(wl2, 16)
    p.g.dma_start(out=killr[:, :], in_=W["aa_kill"]).then_inc(wl2, 16)
    p.g.dma_start(out=cos_sb.ap().rearrange("p (n z) -> p n z", n=NTT),
                  in_=W["aa_cos"].rearrange("(n p) z -> p n z", p=128)
                  ).then_inc(wl2, 16)
    p.g.dma_start(out=sin_sb.ap().rearrange("p (n z) -> p n z", n=NTT),
                  in_=W["aa_sin"].rearrange("(n p) z -> p n z", p=128)
                  ).then_inc(wl2, 16)
    p.t.wait(wl2, wl2.v)
    p.t.wait(dve, dve.v)
    p.t.matmul(ps_s[:, 0:16], ones1[:, :], gr[:, :], start=True, stop=True)
    p.t.matmul(ps_s[:, 32:64], ones1[:, :], killr[:, :], start=True,
               stop=True).then_inc(mm)
    p.v.wait(mm, mm.v)
    p.v.tensor_copy(gain_bc[:, :], ps_s[:, 0:16])
    p.v.tensor_copy(killb[:, :], ps_s[:, 32:64]).then_inc(dve)
    p.s.wait(dve, dve.v)
    p.s.copy(killbc[:, :], killb[:, :]).then_inc(act)
    p.t.wait(dve, dve.v)

    wbD = wb[:, 0:NFC * D]
    wbS = wb[:, 0:NFC * 256]
    dma_w_fm(p, p.g, wbD, W["aa_cqT"], D, D, wl0)
    p.t.wait(wl0, wl0.v)
    for tt in range(NTT):
        p.t.wait(act, act.v)
        emit_mm_tm(p, ps_av, hn_fm, wbD, tt, NFC, 0, D, sem_mm=mm)
        p.s.wait(mm, mm.v)
        p.s.activation(q_tm[:, tt * D:(tt + 1) * D], ps_av[:, :],
                       AF.Copy).then_inc(act)
    p.g.wait(mm, mm.v)
    dma_w_fm(p, p.g, wbS, W["aa_ckT"], D, 256, wl0)
    p.t.wait(wl0, wl0.v)
    for tt in range(NTT):
        p.t.wait(act, act.v)
        emit_mm_tm(p, ps_av[:, 0:256], hn_fm, wbS, tt, NFC, 0, 256, sem_mm=mm)
        p.s.wait(mm, mm.v)
        p.s.activation(k_tm[:, tt * 256:(tt + 1) * 256], ps_av[:, 0:256],
                       AF.Copy).then_inc(act)
    p.v.wait(act, act.v)
    emit_rms_rope(p, nc, W, q_tm, 16, NTT, sems, gain_bc, tag + "q",
                  cos_sb, sin_sb, sq_scr, ssq, rstd)
    emit_rms_rope(p, nc, W, k_tm, 4, NTT, sems, None, tag + "k",
                  cos_sb, sin_sb, sq_scr, ssq, rstd)
    p.t.wait(dve, dve.v)
    for tt in range(NTT):
        r = None
        for ch in range(NFC):
            r = p.t.transpose(ptq[:, ch * 128:(ch + 1) * 128],
                              q_tm[:, tt * D + ch * 128:tt * D + (ch + 1) * 128],
                              ident[:, :])
        r.then_inc(pe)
        p.v.wait(pe, pe.v)
        dst = q_fm.ap().rearrange("p (c k) -> p c k", c=NFC)
        p.v.tensor_copy(dst[:, :, tt * 128:(tt + 1) * 128],
                        ptq.ap().rearrange("p (c k) -> p c k", c=NFC)
                        ).then_inc(dve)
        p.t.wait(dve, dve.v)
    ktf = p.sbuf(tag + "_ktf", [128, 2 * 128], BF16)
    for tt in range(NTT):
        r = None
        for ch in range(2):
            r = p.t.transpose(ptq[:, ch * 128:(ch + 1) * 128],
                              k_tm[:, tt * 256 + ch * 128:tt * 256 + (ch + 1) * 128],
                              ident[:, :])
        r.then_inc(pe)
        p.v.wait(pe, pe.v)
        if tt >= 1:
            p.v.wait(kvs, kvs.v)
        p.v.tensor_copy(ktf.ap().rearrange("p (c k) -> p c k", c=2),
                        ptq.ap().rearrange("p (c k) -> p c k", c=NFC)[:, 0:2, :]
                        ).then_inc(dve)
        p.t.wait(dve, dve.v)
        p.g.wait(dve, dve.v)
        p.g.dma_start(
            out=k_in.ap().rearrange("(c p) t -> p c t",
                                    p=128)[:, :, tt * 128:(tt + 1) * 128],
            in_=ktf.ap().rearrange("p (c k) -> p c k", c=2)).then_inc(kvs, 16)
    p.g.wait(mm, mm.v)
    dma_w_fm(p, p.g, wbS, W["aa_cvT"], D, 256, wl0)
    p.t.wait(wl0, wl0.v)
    vtile = p.sbuf(tag + "_vt", [128, 2 * 256], BF16)
    kvs0 = kvs.v
    for tt in range(NTT):
        p.t.wait(act, act.v)
        emit_mm_tm(p, ps_av[:, 0:256], hn_fm, wbS, tt, NFC, 0, 256, sem_mm=mm)
        p.s.wait(mm, mm.v)
        if tt >= 2:
            p.s.wait(kvs, kvs0 + 16 * (tt - 1))
        p.s.activation(vtile[:, (tt % 2) * 256:(tt % 2 + 1) * 256],
                       ps_av[:, 0:256], AF.Copy).then_inc(act)
        p.g.wait(act, act.v)
        p.g.dma_start(out=v_in[tt * 128:(tt + 1) * 128, :],
                      in_=vtile[:, (tt % 2) * 256:(tt % 2 + 1) * 256]
                      ).then_inc(kvs, 16)
    p.g.wait(kvs, kvs.v)
    p.g.collective_compute(
        "AllGather", ALU.bypass, replica_groups=[list(range(NCORES))],
        ins=[k_in.ap().opt()], outs=[k_all.ap().opt()]).inc_default(cc)
    p.g.collective_compute(
        "AllGather", ALU.bypass, replica_groups=[list(range(NCORES))],
        ins=[v_in.ap().opt()], outs=[v_all.ap().opt()]).inc_default(cc)
    p.g.wait(cc, cc.v)

    mk = p.sbuf(tag + "_mk", [128, 4 * 512], BF16)
    p.g.dma_start(out=mk.ap().rearrange("p (o j) -> p o j", o=4),
                  in_=W["aa_mask"].rearrange("o p j -> p o j")
                  ).then_inc(wl1, 16)

    acc = p.sbuf(tag + "_acc", [128, 4 * 1024], F32)
    o_tm = hn_fm
    pb = cos_sb
    ptfm = sin_sb
    kb2 = [p.sbuf(tag + "_kb0", [128, 4 * 512], BF16),
           p.sbuf(tag + "_kb1", [128, 4 * 512], BF16)]
    vb2 = [p.sbuf(tag + "_vb0", [128, 4 * 256], BF16),
           p.sbuf(tag + "_vb1", [128, 4 * 256], BF16)]
    mA = p.sbuf(tag + "_mA", [128, 64], F32)
    mC = p.sbuf(tag + "_mC", [128, 64], F32)
    lA = [p.sbuf(tag + "_lA0", [128, 64], F32),
          p.sbuf(tag + "_lA1", [128, 64], F32)]
    srow = p.sbuf(tag + "_srow", [128, 4], F32)
    srowc = p.sbuf(tag + "_srowc", [128, 4], F32)
    nm2a = p.sbuf(tag + "_nm2a", [128, 4], F32)
    nm2 = p.sbuf(tag + "_nm2", [128, 4], F32)
    nm2k = p.sbuf(tag + "_nm2k", [128, 4], F32)
    ddm = p.sbuf(tag + "_ddm", [128, 4], F32)
    alp = p.sbuf(tag + "_alp", [128, 4], F32)
    rsum = p.sbuf(tag + "_rs", [128, 4], F32)
    rcp = p.sbuf(tag + "_rcp", [128, 64], F32)

    p.v.wait(cc, cc.v)
    p.t.wait(wl1, wl1.v)
    nblk = 0
    for qslot in range(2):
        for b in range(B):
            qt0 = qslot * 8 + b * 4
            off = qslot * 1024 + b * 512
            p.v.wait(act, act.v)
            p.v.memset(mA[:, :], -3.0e30)
            p.v.memset(lA[0][:, :], 0.0)
            p.v.memset(acc[:, :], 0.0).then_inc(dve)
            p.s.wait(dve, dve.v)
            p.s.copy(mC[:, :], mA[:, :]).then_inc(act)
            lpi = 0
            blocks = [("own", off, True, None)]
            for sk in range(NSTRIPE):
                kr = sk if sk < 8 else 15 - sk
                koff = (0 if sk < 8 else 1024) + b * 512
                blocks.append(("ag", (kr, koff), False, qslot * 16 + sk))
            for srcw, loc, diag, kidx in blocks:
                kb, vb = kb2[nblk % 2], vb2[nblk % 2]
                kl, vl = kld[nblk % 2], vld[nblk % 2]
                if nblk >= 2:
                    p.g.wait(mm, mm.v)
                if srcw == "own":
                    kap, koff = k_in[0:256], loc
                    vap = v_in[loc:loc + 512]
                else:
                    kr, koff = loc
                    kap = k_all[kr * 256:(kr + 1) * 256]
                    vap = v_all[kr * T + koff:kr * T + koff + 512]
                ksrc = kap.rearrange("(c p) t -> p c t",
                                     p=64)[:, :, koff:koff + 512]
                p.g.dma_start(
                    out=kb.ap().rearrange("p (c j) -> p c j", c=4)[0:64],
                    in_=ksrc).then_inc(kl, 16)
                p.g.dma_start(
                    out=kb.ap().rearrange("p (c j) -> p c j", c=4)[64:128],
                    in_=ksrc).then_inc(kl, 16)
                p.g.dma_start(
                    out=vb.ap().rearrange("p (j d) -> p j d", j=4),
                    in_=vap.rearrange("(j p) d -> p j d", p=128)
                    ).then_inc(vl, 16)
                p.t.wait(kl, kl.v)
                p.t.wait(vl, vl.v)
                for j in range(4):
                    tt = qt0 + j
                    for hg in range(4):
                        mcol = j * 16 + hg * 4
                        p.t.wait(act, act.v)
                        r = None
                        for hl in range(4):
                            h = hg * 4 + hl
                            ch = h // 2
                            hb = (h % 2) * 64
                            kh = h // 4
                            r = p.t.matmul(
                                ps_s[:, hl * 512:(hl + 1) * 512],
                                q_fm[hb:hb + 64,
                                     ch * T + tt * 128:ch * T + (tt + 1) * 128],
                                kb[hb:hb + 64, kh * 512:(kh + 1) * 512],
                                start=True, stop=True)
                        if diag:
                            for hl in range(4):
                                r = p.t.matmul(
                                    ps_s[:, hl * 512:(hl + 1) * 512],
                                    ident[:, :], mk[:, j * 512:(j + 1) * 512],
                                    start=False, stop=True,
                                    skip_group_check=True)
                        r.then_inc(mm)
                        p.v.wait(mm, mm.v)
                        p.v.reduce_max(
                            srow[:, 0:4],
                            ps_s.ap().rearrange("p (h k) -> p h k", h=4),
                            axis=AX.X).then_inc(dve)
                        p.s.wait(dve, dve.v)
                        p.s.copy(srowc[:, :], srow[:, 0:4]).then_inc(act)
                        p.v.wait(act, act.v)
                        p.v.tensor_max(mA[:, mcol:mcol + 4],
                                       mC[:, mcol:mcol + 4],
                                       srowc[:, :]).then_inc(dve)
                        p.s.wait(dve, dve.v)
                        p.s.mul(nm2a[:, :], mA[:, mcol:mcol + 4],
                                -SCL).then_inc(act)
                        p.v.wait(act, act.v)
                        r = p.v.tensor_copy(nm2[:, :], nm2a[:, :])
                        if kidx is not None:
                            r = p.v.tensor_scalar_add(
                                nm2k[:, :], nm2a[:, :],
                                killbc[:, kidx:kidx + 1])
                        r = p.v.scalar_tensor_tensor(
                            ddm[:, :], nm2a[:, :], 1.0 / SCL,
                            mC[:, mcol:mcol + 4], ALU.mult, ALU.add)
                        r.then_inc(dve)
                        pbias = nm2k if kidx is not None else nm2
                        p.s.wait(dve, dve.v)
                        p.s.activation(alp[:, :], ddm[:, :], AF.Exp,
                                       scale=SCL)
                        r = None
                        for hl in range(4):
                            r = p.s.activation(
                                pb[:, hl * 512:(hl + 1) * 512],
                                ps_s[:, hl * 512:(hl + 1) * 512], AF.Exp,
                                scale=SCL, bias=pbias[:, hl:hl + 1],
                                accum_out=rsum[:, hl:hl + 1])
                        r.then_inc(act)
                        p.s.copy(mC[:, mcol:mcol + 4],
                                 mA[:, mcol:mcol + 4]).then_inc(act)
                        lp, lc = lA[lpi], lA[1 - lpi]
                        p.v.wait(act, act.v)
                        for hl in range(4):
                            p.v.scalar_tensor_tensor(
                                lc[:, mcol + hl:mcol + hl + 1],
                                lp[:, mcol + hl:mcol + hl + 1],
                                alp[:, hl:hl + 1], rsum[:, hl:hl + 1],
                                ALU.mult, ALU.add)
                        p.events[-1][1].then_inc(dve)
                        p.t.wait(act, act.v)
                        p.t.wait(dve, dve.v)
                        for half in range(2):
                            dstp = ptq if half == 0 else ptp
                            r = None
                            for hl in range(2):
                                hh = half * 2 + hl
                                for jj in range(4):
                                    r = p.t.transpose(
                                        dstp[:, (hl * 4 + jj) * 128:
                                             (hl * 4 + jj + 1) * 128],
                                        pb[:, hh * 512 + jj * 128:
                                           hh * 512 + (jj + 1) * 128],
                                        ident[:, :])
                            r.then_inc(pe)
                        p.v.wait(pe, pe.v)
                        p.v.wait(mm, mm.v)
                        p.v.tensor_copy(ptfm[:, 0:1024], ptq[:, :])
                        p.v.tensor_copy(ptfm[:, 1024:2048],
                                        ptp[:, :]).then_inc(dve)
                        p.t.wait(dve, dve.v)
                        r = None
                        for hl in range(4):
                            h = hg * 4 + hl
                            kh = h // 4
                            for jj in range(4):
                                r = p.t.matmul(
                                    ps_av[:, hl * 64:(hl + 1) * 64],
                                    ptfm[:, hl * 512 + jj * 128:
                                         hl * 512 + (jj + 1) * 128],
                                    vb[:, jj * 256 + kh * 64:
                                       jj * 256 + (kh + 1) * 64],
                                    start=(jj == 0), stop=(jj == 3))
                        r.then_inc(mm)
                        p.v.wait(mm, mm.v)
                        for hl in range(4):
                            h = hg * 4 + hl
                            p.v.scalar_tensor_tensor(
                                acc[:, j * 1024 + h * 64:j * 1024 + (h + 1) * 64],
                                acc[:, j * 1024 + h * 64:j * 1024 + (h + 1) * 64],
                                alp[:, hl:hl + 1],
                                ps_av[:, hl * 64:(hl + 1) * 64],
                                ALU.mult, ALU.add)
                        p.events[-1][1].then_inc(dve)
                        p.t.wait(dve, dve.v)
                lpi = 1 - lpi
                nblk += 1
            p.v.reciprocal(rcp[:, 0:64], lA[lpi][:, 0:64]).then_inc(dve)
            p.s.wait(dve, dve.v)
            p.s.copy(rcp[:, 0:64], rcp[:, 0:64]).then_inc(act)
            p.v.wait(act, act.v)
            for j in range(4):
                tt = qt0 + j
                for h in range(16):
                    p.v.tensor_scalar_mul(
                        o_tm[:, tt * D + h * 64:tt * D + (h + 1) * 64],
                        acc[:, j * 1024 + h * 64:j * 1024 + (h + 1) * 64],
                        rcp[:, j * 16 + h:j * 16 + h + 1])
            p.events[-1][1].then_inc(dve)

    p.g.wait(mm, mm.v)
    dma_w_fm(p, p.g, wbD, W["aa_cprojT"], D, D, wl0)
    p.t.wait(wl0, wl0.v)
    p.q.wait(dve, dve.v)
    ofm = p.sbuf(tag + "_ofm", [128, D], BF16)
    for tt in range(NTT):
        p.t.wait(dve, dve.v)
        r = None
        for ch in range(NFC):
            r = p.t.transpose(ptq[:, ch * 128:(ch + 1) * 128],
                              o_tm[:, tt * D + ch * 128:tt * D + (ch + 1) * 128],
                              ident[:, :])
        r.then_inc(pe)
        p.v.wait(pe, pe.v)
        p.v.tensor_copy(ofm[:, :], ptq[:, :]).then_inc(dve)
        p.t.wait(dve, dve.v)
        r = None
        for h2 in range(2):
            for ch in range(NFC):
                r = p.t.matmul(ps_av[:, h2 * 512:(h2 + 1) * 512],
                               ofm[:, ch * 128:(ch + 1) * 128],
                               wbD[:, ch * D + h2 * 512:ch * D + (h2 + 1) * 512],
                               start=(ch == 0), stop=(ch == NFC - 1))
        r.then_inc(mm)
        sl = xldS[tt % 2]
        if tt >= 2:
            p.q.wait(dve, dve.v - 1)
        p.q.dma_start(out=xt2[tt % 2][:, :],
                      in_=uin[tt * 128:(tt + 1) * 128, :]).then_inc(sl, 16)
        p.v.wait(sl, sl.v)
        p.v.wait(mm, mm.v)
        p.v.tensor_add(xt2[tt % 2][:, :], xt2[tt % 2][:, :],
                       ps_av[:, :]).then_inc(dve)
        p.t.wait(dve, dve.v)
        p.q.wait(dve, dve.v)
        p.q.dma_start(out=vmid[tt * 128:(tt + 1) * 128, :],
                      in_=xt2[tt % 2][:, :]).then_inc(vstore, 16)
    p.q.wait(vstore, vstore.v)

    st2 = mk_stats(p, tag + "2", df=1)
    st2["sq"], st2["sq2"] = st["sq"], st["sq2"]
    _, rstd2 = emit_stream_ln(p, vmid.ap(), st2, sems, tag + "2", xt2=xt2)
    emit_stream_norm_transpose(p, vmid.ap(), st2, hn_fm, ident, xt2,
                               [ptp[:, 0:D], ptq[:, 0:D]], sems,
                               rstd2, gamma=g2, beta=b2, hn2=p._last_hn2)

    a_grp = q_tm
    y_mlp = acc
    ut2 = [sq_scr, st["sq"]]
    uld = [p.sem(tag + "_uld0"), p.sem(tag + "_uld1")]
    for tg in range(4):
        c0 = tg * 512
        for half in range(2):
            p.g.wait(mm, mm.v)
            wbF = wb[:, 0:NFC * 1536]
            dma_w_fm(p, p.g, wbF, W["aa_fcT"], D, 3072, wl0,
                     col0=half * 1536, ncols=1536)
            p.t.wait(wl0, wl0.v)
            for ol in range(12):
                och = half * 12 + ol
                p.t.wait(act, act.v)
                emit_mm_fm(p, ps_s[:, 0:512], wbF, ol, hn_fm, c0, c0 + 512,
                           NFC, sem_mm=mm)
                p.s.wait(mm, mm.v)
                p.s.activation(fm_ap(a_grp, och, 0, 512, 512),
                               ps_s[:, 0:512], AF.Silu).then_inc(act)
        p.v.wait(act, act.v)
        p.v.memset(y_mlp[:, :], 0.0).then_inc(dve)
        for half in range(2):
            p.g.wait(mm, mm.v)
            src2 = W["aa_projT"].rearrange("(c p) j -> p c j", p=128)
            p.g.dma_start(out=wb.ap().rearrange("p (c j) -> p c j", c=12),
                          in_=src2[:, half * 12:(half + 1) * 12, :]
                          ).then_inc(wl0, 16)
            p.t.wait(wl0, wl0.v)
            p.t.wait(act, act.v)
            p.t.wait(dve, dve.v)
            for j in range(4):
                r = None
                for h2 in range(2):
                    for cl in range(12):
                        ch = half * 12 + cl
                        r = p.t.matmul(
                            ps_av[:, h2 * 512:(h2 + 1) * 512],
                            fm_ap(a_grp, ch, j * 128, (j + 1) * 128, 512),
                            wb[:, cl * D + h2 * 512:cl * D + (h2 + 1) * 512],
                            start=(cl == 0), stop=(cl == 11))
                r.then_inc(mm)
                p.v.wait(mm, mm.v)
                p.v.tensor_add(y_mlp[:, j * D:(j + 1) * D],
                               y_mlp[:, j * D:(j + 1) * D],
                               ps_av[:, :]).then_inc(dve)
                p.t.wait(dve, dve.v)
        for j in range(4):
            tt = tg * 4 + j
            sl = xldS[j % 2]
            ul = uld[j % 2]
            if j >= 2:
                p.q.wait(dve, dve.v - 1)
            else:
                p.q.wait(dve, dve.v)
            p.q.dma_start(out=xt2[j % 2][:, :],
                          in_=xin[tt * 128:(tt + 1) * 128, :]).then_inc(sl, 16)
            p.q.dma_start(out=ut2[j % 2][:, :],
                          in_=vmid[tt * 128:(tt + 1) * 128, :]).then_inc(ul, 16)
            p.v.wait(sl, sl.v)
            p.v.wait(ul, ul.v)
            p.v.tensor_add(xt2[j % 2][:, :], xt2[j % 2][:, :],
                           ut2[j % 2][:, :])
            p.v.tensor_add(xt2[j % 2][:, :], xt2[j % 2][:, :],
                           y_mlp[:, j * D:(j + 1) * D]).then_inc(dve)
            p.q.wait(dve, dve.v)
            p.q.dma_start(out=xout[tt * 128:(tt + 1) * 128, :],
                          in_=xt2[j % 2][:, :]).then_inc(sto, 16)
    return sto.v


def build_l0attn_nc(dbg=False):
    nc = mk_nc()
    p = Prog(nc)
    xin = nc.dram_tensor("xin", [T, D], F32, kind="ExternalInput").ap()
    uin = nc.dram_tensor("uin", [T, D], F32, kind="ExternalInput").ap()
    xout = nc.dram_tensor("xout", [T, D], F32, kind="ExternalOutput").ap()
    sems = mk_sems(p)
    W = declare_l0attn(nc)
    ident = load_ident(p, nc, sems)
    emit_l0attn(p, nc, xin, uin, xout, W, ident, sems, dbg=dbg)
    p.q.wait(sems["sto"])
    return p.finalize()


from concourse.aot_env import allow_ici_all_to_all

D_INNER = 2048
NH = 32
MYH = 4
D_PROJ = 4384


def prep_mamba_host(inputs, core):
    W = {"mb_identf": np.eye(128, dtype=np.float32),
         "mb_inpT": np.ascontiguousarray(inputs["L0_attn_a_in_proj_w"].T),
         "mb_outT": np.ascontiguousarray(inputs["L0_attn_a_out_proj_w"].T),
         "mb_mnw": np.ascontiguousarray(
             inputs["L0_attn_a_mnorm_w"].reshape(8, 128).T),
         "mb_zeros8": np.zeros((128, 8), np.float32),
         "mb_bnw": np.ascontiguousarray(
             inputs["L0_attn_a_Bn_w"].reshape(128, 1)),
         "mb_cnw": np.ascontiguousarray(
             inputs["L0_attn_a_Cn_w"].reshape(128, 1)),
         "mb_dtb": np.ascontiguousarray(
             inputs["L0_attn_a_dt_bias"].reshape(1, 32)),
         "mb_alog": np.ascontiguousarray(
             inputs["L0_attn_a_A_log"][core * MYH:(core + 1) * MYH
                                       ].reshape(MYH, 1)),
         "mb_dexp": np.ascontiguousarray(
             np.repeat(inputs["L0_attn_a_D_param"], 64).reshape(1, 2048))}
    e2 = np.zeros((64, MYH * 64), np.float32)
    for h in range(MYH):
        e2[np.arange(64), h * 64 + np.arange(64)] = 1.0
    W["mb_e2"] = e2
    e4 = np.zeros((MYH, MYH * 64), np.float32)
    for h in range(MYH):
        e4[h, h * 64:(h + 1) * 64] = 1.0
    W["mb_e4"] = e4
    lt = (np.arange(64)[:, None] >= np.arange(64)[None, :]).astype(np.float32)
    W["mb_lt"] = lt
    return W


def declare_mamba(nc):
    W = {}
    for nm, shp in (("mb_identf", [128, 128]),
                    ("mb_inpT", [D, D_PROJ]), ("mb_outT", [D_INNER, D]),
                    ("mb_mnw", [128, 8]), ("mb_zeros8", [128, 8]),
                    ("mb_bnw", [128, 1]), ("mb_cnw", [128, 1]),
                    ("mb_dtb", [1, 32]), ("mb_alog", [MYH, 1]),
                    ("mb_dexp", [1, 2048]), ("mb_e2", [64, MYH * 64]),
                    ("mb_e4", [MYH, MYH * 64]), ("mb_lt", [64, 64])):
        W[nm] = nc.dram_tensor(nm, shp, F32, kind="ExternalInput").ap()
    return W


def chunk_row(cg, b):
    s = cg // 8
    r = s if s < 8 else 15 - s
    slot = 0 if s < 8 else 1
    return r * T + slot * 1024 + b * 512 + (cg % 8) * 64


def emit_mamba(p, nc, xin, uout, W, ident, sems, tag="mb"):
    xldS, dve, act, pe, ev, mm, sto = (
        sems[k] for k in ("xldS", "dve", "act", "pe", "ev", "mm", "sto"))
    wl0, wl1, wl2 = sems["wlS"]
    cc = p.sem(tag + "_cc")
    exs = p.sem(tag + "_exs")
    if getattr(p, "want_marks", False):
        mkb = p.sbuf(tag + "_mkb", [1, 64], F32)
        mark_dram = nc.dram_tensor(
            "marks", [1, 64], F32,
            kind="ExternalOutput" if getattr(p, "marks_out", False)
            else "Internal")
        mks = p.sem(tag + "_mks")

        def mark(idx):
            p.v.memset(mkb[0:1, idx:idx + 1], float(idx + 1)).then_inc(dve)
            p.q.wait(dve, dve.v)
            p.q.dma_start(out=mark_dram.ap()[0:1, idx:idx + 1],
                          in_=mkb[0:1, idx:idx + 1]).then_inc(mks, 16)
    else:
        def mark(idx):
            pass
    p.mamba_mark = mark

    z_dram = nc.dram_tensor(tag + "_z", [T, D_INNER], BF16)
    xs_in = nc.dram_tensor(tag + "_xsin", [NCORES * T, 256], BF16)
    xs_hd = nc.dram_tensor(tag + "_xshd", [NCORES * T, 256], BF16)
    dt_in = nc.dram_tensor(tag + "_dtin", [NCORES * T, MYH], F32)
    dt_hd = nc.dram_tensor(tag + "_dthd", [NCORES * T, MYH], F32)
    bc_in = nc.dram_tensor(tag + "_bcin", [T, 256], F32)
    bc_all = nc.dram_tensor(tag + "_bcall", [NCORES * T, 256], F32)
    y_in = nc.dram_tensor(tag + "_yin", [NCORES * T, 256], BF16)
    y_hd = nc.dram_tensor(tag + "_yhd", [NCORES * T, 256], BF16)
    p.mamba_dbg = dict(y_hd=y_hd, xs_hd=xs_hd, dt_hd=dt_hd, bc_all=bc_all,
                       z_dram=z_dram, y_in=y_in)

    st = mk_stats(p, tag)
    vare = p.sbuf(tag + "_varecp", [128, NTT], F32)
    hm_fm = p.sbuf(tag + "_hmfm", [128, NFC * T], BF16)
    xt2, rstd_ready = emit_stream_ln(p, xin, st, sems, tag, also_vare=vare)
    iv = p.sbuf(tag + "_iv", [128, NTT], F32)
    rstd2 = p.sbuf(tag + "_rstd2", [128, NTT], F32)
    p.v.wait(act, rstd_ready)
    p.v.reciprocal(iv[:, :], vare[:, :]).then_inc(dve)
    p.s.wait(dve, dve.v)
    p.s.activation(rstd2[:, :], iv[:, :], AF.Copy, scale=-LN_EPS,
                   bias=1.0 + LN_EPS)
    p.s.activation(rstd2[:, :], rstd2[:, :], AF.Ln)
    p.s.activation(rstd2[:, :], rstd2[:, :], AF.Exp, scale=-0.5).then_inc(act)
    p.v.wait(act, act.v)
    p.v.tensor_mul(rstd2[:, :], rstd2[:, :], st["rstd"][:, :]).then_inc(dve)
    p.s.wait(dve, dve.v)
    p.s.copy(rstd2[:, :], rstd2[:, :]).then_inc(act)
    mark(0)
    mnw = p.sbuf(tag + "_mnws", [128, 8], F32)
    zr8 = p.sbuf(tag + "_zr8", [128, 8], F32)
    p.q.dma_start(out=mnw[:, :], in_=W["mb_mnw"]).then_inc(sems["sld"], 16)
    p.q.dma_start(out=zr8[:, :], in_=W["mb_zeros8"]).then_inc(sems["sld"], 16)
    p.v.wait(sems["sld"])
    st2 = dict(st)
    st2["rstd"] = rstd2
    ptp = p.psum(tag + "_ptp", [128, D], BF16)
    ptq = p.psum(tag + "_ptq", [128, D], BF16)
    emit_stream_norm_transpose(p, xin, st2, hm_fm, ident, xt2,
                               [ptp[:, 0:D], ptq[:, 0:D]], sems,
                               act.v, gamma=mnw, beta=zr8)

    wb = p.sbuf(tag + "_wb", [128, NFC * 512], BF16)
    ps1 = p.psum(tag + "_ps1", [128, 512])
    ps2 = p.psum(tag + "_ps2", [128, 512])
    ps3 = p.psum(tag + "_ps3", [128, 512])
    ps4 = p.psum(tag + "_ps4", [128, 512])
    ps5 = p.psum(tag + "_ps5", [128, 512])

    et = p.sbuf(tag + "_et", [128, 2 * 512], BF16)
    bcdt_t = p.sbuf(tag + "_bcdt", [128, 288], F32)
    bco = p.sbuf(tag + "_bco", [128, 288], F32)
    bss = p.sbuf(tag + "_bss", [128, 4], F32)
    brs = p.sbuf(tag + "_brs", [128, 4], F32)
    dtbb = p.sbuf(tag + "_dtbb", [128, 32], F32)
    tdt = p.sbuf(tag + "_tdt", [128, 32], F32)
    l1t = p.sbuf(tag + "_l1t", [128, 32], F32)
    l1m = p.sbuf(tag + "_l1m", [128, 32], F32)
    l1p = p.sbuf(tag + "_l1p", [128, 32], F32)
    ones1 = p.sbuf(tag + "_ones1", [1, 128], BF16)
    dtbr = p.sbuf(tag + "_dtbr", [1, 32], BF16)
    p.v.memset(ones1[:, :], 1.0).then_inc(dve)
    p.g.dma_start(out=dtbr[:, :], in_=W["mb_dtb"]).then_inc(wl2, 16)
    p.t.wait(wl2, wl2.v)
    p.t.wait(dve, dve.v)
    p.t.matmul(ps5[:, 0:32], ones1[:, :], dtbr[:, :], start=True,
               stop=True).then_inc(mm)
    p.v.wait(mm, mm.v)
    p.v.tensor_copy(dtbb[:, :], ps5[:, 0:32]).then_inc(dve)
    p.t.wait(dve, dve.v)

    for j in range(9):
        ncols = 512 if j < 8 else 288
        p.g.wait(mm, mm.v)
        dma_w_fm(p, p.g, wb[:, 0:NFC * ncols], W["mb_inpT"], D, D_PROJ, wl0,
                 col0=j * 512, ncols=ncols)
        p.t.wait(wl0, wl0.v)
        for tt in range(NTT):
            p.t.wait(act, act.v)
            p.t.wait(dve, dve.v)
            emit_mm_tm(p, ps1[:, 0:ncols], hm_fm, wb[:, 0:NFC * ncols], tt,
                       NFC, 0, ncols, sem_mm=mm)
            p.s.wait(mm, mm.v)
            if j < 8:
                sl = (tt % 2) * 512
                if tt >= 2:
                    p.s.wait(exs, exs.v - 16)
                p.s.activation(et[:, sl:sl + 512], ps1[:, 0:512],
                               AF.Copy).then_inc(act)
                p.g.wait(act, act.v)
                if j < 4:
                    dst = z_dram[tt * 128:(tt + 1) * 128,
                                 j * 512:(j + 1) * 512]
                    p.g.dma_start(out=dst, in_=et[:, sl:sl + 512]
                                  ).then_inc(exs, 16)
                else:
                    jj = j - 4
                    dst = xs_in.ap().rearrange("(blk t) d -> blk t d",
                                               blk=NCORES)
                    dst = dst[2 * jj:2 * jj + 2, tt * 128:(tt + 1) * 128, :]
                    dst = dst.rearrange("blk t d -> t blk d")
                    src = et[:, sl:sl + 512].rearrange("p (b d) -> p b d", b=2)
                    p.g.dma_start(out=dst, in_=src).then_inc(exs, 16)
            else:
                p.s.activation(bcdt_t[:, :], ps1[:, 0:288], AF.Copy)
                p.s.activation(st["sq"][:, 0:256], bcdt_t[:, 0:256],
                               AF.Square).then_inc(act)
                p.v.wait(act, act.v)
                p.v.reduce_sum(bss[:, 0:2],
                               st["sq"][:, 0:256].rearrange(
                                   "p (g n) -> p g n", g=2),
                               axis=AX.X).then_inc(dve)
                p.s.wait(dve, dve.v)
                p.s.activation(brs[:, 0:2], bss[:, 0:2], AF.Copy,
                               scale=1.0 / 128.0, bias=LN_EPS)
                p.s.activation(brs[:, 0:2], brs[:, 0:2], AF.Ln)
                p.s.activation(brs[:, 0:2], brs[:, 0:2], AF.Exp,
                               scale=-0.5).then_inc(act)
                p.v.wait(act, act.v)
                p.v.wait(exs, exs.v)
                p.v.tensor_scalar_mul(bco[:, 0:128], bcdt_t[:, 0:128],
                                      brs[:, 0:1])
                p.v.tensor_scalar_mul(bco[:, 128:256], bcdt_t[:, 128:256],
                                      brs[:, 1:2])
                p.v.tensor_add(tdt[:, :], bcdt_t[:, 256:288],
                               dtbb[:, :]).then_inc(dve)
                p.s.wait(dve, dve.v)
                p.s.activation(l1m[:, :], tdt[:, :], AF.Copy, scale=-1.0)
                p.s.copy(l1p[:, :], tdt[:, :]).then_inc(act)
                p.v.wait(act, act.v)
                p.v.tensor_max(l1t[:, :], l1m[:, :], l1p[:, :]).then_inc(dve)
                p.s.wait(dve, dve.v)
                p.s.activation(l1t[:, :], l1t[:, :], AF.Exp, scale=-1.0)
                p.s.activation(l1t[:, :], l1t[:, :], AF.Copy, bias=1.0)
                p.s.activation(l1t[:, :], l1t[:, :], AF.Ln).then_inc(act)
                p.v.wait(act, act.v)
                p.v.scalar_tensor_tensor(bco[:, 256:288], l1p[:, :], 0.0,
                                         l1t[:, :], op0=ALU.max,
                                         op1=ALU.add).then_inc(dve)
                p.g.wait(dve, dve.v)
                p.g.dma_start(out=bc_in[tt * 128:(tt + 1) * 128, :],
                              in_=bco[:, 0:256]).then_inc(exs, 16)
                dst = dt_in.ap().rearrange("(blk t) h -> blk t h", blk=NCORES)
                dst = dst[:, tt * 128:(tt + 1) * 128, :]
                dst = dst.rearrange("blk t h -> t blk h")
                src = bco[:, 256:288].rearrange("p (blk h) -> p blk h",
                                                blk=NCORES)
                p.g.dma_start(out=dst, in_=src).then_inc(exs, 16)
    mark(1)
    p.g.wait(exs, exs.v)
    with allow_ici_all_to_all():
        p.g.collective_compute(
            "AllToAll", ALU.bypass, replica_groups=[list(range(NCORES))],
            ins=[xs_in.ap().opt()], outs=[xs_hd.ap().opt()]).inc_default(cc)
        p.g.collective_compute(
            "AllToAll", ALU.bypass, replica_groups=[list(range(NCORES))],
            ins=[dt_in.ap().opt()], outs=[dt_hd.ap().opt()]).inc_default(cc)
    p.g.collective_compute(
        "AllGather", ALU.bypass, replica_groups=[list(range(NCORES))],
        ins=[bc_in.ap().opt()], outs=[bc_all.ap().opt()]).inc_default(cc)
    p.g.wait(cc, cc.v)
    mark(2)

    e2c = p.sbuf(tag + "_e2c", [64, MYH * 64], BF16)
    e4c = p.sbuf(tag + "_e4c", [MYH, MYH * 64], BF16)
    ltc = p.sbuf(tag + "_ltc", [64, 64], BF16)
    alg = p.sbuf(tag + "_alg", [MYH, 1], F32)
    aneg = p.sbuf(tag + "_aneg", [MYH, 1], F32)
    bnw = p.sbuf(tag + "_bnws", [128, 1], F32)
    cnw = p.sbuf(tag + "_cnws", [128, 1], F32)
    p.g.dma_start(out=e2c[:, :], in_=W["mb_e2"]).then_inc(wl2, 16)
    p.g.dma_start(out=e4c[:, :], in_=W["mb_e4"]).then_inc(wl2, 16)
    p.g.dma_start(out=ltc[:, :], in_=W["mb_lt"]).then_inc(wl2, 16)
    p.g.dma_start(out=alg[:, :], in_=W["mb_alog"]).then_inc(wl2, 16)
    p.q.dma_start(out=bnw[:, :], in_=W["mb_bnw"]).then_inc(sems["sld"], 16)
    p.q.dma_start(out=cnw[:, :], in_=W["mb_cnw"]).then_inc(sems["sld"], 16)
    p.s.wait(wl2, wl2.v)
    p.s.activation(aneg[:, :], alg[:, :], AF.Exp)
    p.s.mul(aneg[:, :], aneg[:, :], -1.0).then_inc(act)
    mark(3)

    xs_str = p.sbuf(tag + "_xsstr", [64, 8 * 256], BF16)
    bc_str = p.sbuf(tag + "_bcstr", [128, 4 * 256], BF16)
    dt_str = p.sbuf(tag + "_dtstr", [128, 4 * MYH], F32)
    bfm = p.sbuf(tag + "_bfm", [128, 512], BF16)
    cfm = p.sbuf(tag + "_cfm", [128, 512], BF16)
    dtfm = p.sbuf(tag + "_dtfm", [MYH, 512], F32)
    afm = p.sbuf(tag + "_afm", [MYH, 512], F32)
    cum_s = p.sbuf(tag + "_cums", [MYH, 512], F32)
    cumng = p.sbuf(tag + "_cumng", [MYH, 512], F32)
    ones64 = p.sbuf(tag + "_o64", [MYH, 64], F32)
    ecls = p.sbuf(tag + "_ecls", [MYH, 8], F32)
    cumflat = p.sbuf(tag + "_cumflat", [1, 2048], F32)
    eclflat = p.sbuf(tag + "_eclflat", [1, 2048], F32)
    elastflat = p.sbuf(tag + "_elflat", [1, 2048], F32)
    eclev = p.sbuf(tag + "_eclev", [8, 256], F32)
    identf = p.sbuf(tag + "_idfs", [128, 128], F32)
    gtm = p.sbuf(tag + "_gtm", [64, 64], BF16)
    dmin = p.sbuf(tag + "_dmin", [64, MYH * 64], F32)
    exd = p.sbuf(tag + "_exd", [64, MYH * 64], BF16)
    gm = p.sbuf(tag + "_gm", [64, MYH * 64], BF16)
    efb = p.sbuf(tag + "_efb", [64, MYH * 64], F32)
    xdt1 = p.sbuf(tag + "_xdt1", [64, MYH * 64], BF16)
    xdt2 = p.sbuf(tag + "_xdt2", [64, MYH * 64], BF16)
    bts = p.sbuf(tag + "_bts", [64, 128], BF16)
    eplb = p.sbuf(tag + "_eplb", [64, MYH * 64], BF16)
    yosb = p.sbuf(tag + "_yosb", [64, MYH * 64], F32)
    ysb = p.sbuf(tag + "_ysb", [64, 8 * 256], BF16)
    Nst = p.sbuf(tag + "_Nst", [128, 256], F32)
    Nsc = p.sbuf(tag + "_Nsc", [128, 256], F32)
    yosc = p.sbuf(tag + "_yosc", [64, 256], F32)
    xdt1c = p.sbuf(tag + "_xdt1c", [64, 256], BF16)
    dtfmc = p.sbuf(tag + "_dtfmc", [MYH, 512], F32)
    afmc = p.sbuf(tag + "_afmc", [MYH, 512], F32)
    Nbf = p.sbuf(tag + "_Nbf", [128, 256], BF16)
    ystore = p.sem(tag + "_yst")
    onesf = p.sbuf(tag + "_onesf", [1, 64], F32)
    ones1f = p.sbuf(tag + "_ones1f", [1, 128], F32)
    e4f = p.sbuf(tag + "_e4f", [MYH, MYH * 64], F32)
    ptf = p.psum(tag + "_ptf", [128, 128], F32)
    cumstr_dram = nc.dram_tensor(tag + "_cumsd", [MYH, 512], F32)
    ecl8_dram = nc.dram_tensor(tag + "_ecl8d", [8, 256], F32)
    p.q.dma_start(out=e4f[:, :], in_=W["mb_e4"]).then_inc(sems["sld"], 16)
    p.q.dma_start(out=identf[:, :],
                  in_=W["mb_identf"]).then_inc(sems["sld"], 16)
    p.v.memset(onesf[:, :], 1.0)
    p.v.memset(ones1f[:, :], 1.0)
    p.v.memset(ones64[:, :], 1.0).then_inc(dve)
    p.t.wait(dve, dve.v)
    p.t.wait(sems["sld"])

    for b in range(B):
        p.g.wait(cc, cc.v)
        p.v.wait(act, act.v)
        p.v.wait(ystore, ystore.v)
        p.v.memset(Nst[:, :], 0.0)
        p.v.memset(ysb[:, :], 0.0).then_inc(dve)
        p.s.wait(dve, dve.v)
        p.s.activation(Nbf[:, :], Nst[:, :], AF.Copy).then_inc(act)
        p.t.wait(act, act.v)
        for c in range(128):
            s = c // 8
            r = s if s < 8 else 15 - s
            slot = 0 if s < 8 else 1
            cs = (c % 8) * 256
            cl = (c % 8) * 64
            if c % 8 == 0:
                mark(8 + b * 16 + s)
                base = r * T + slot * 1024 + b * 512
                p.g.wait(mm, mm.v)
                p.g.wait(dve, dve.v)
                p.g.dma_start(out=xs_str.ap().rearrange(
                    "p (n d) -> p n d", n=8),
                    in_=xs_hd[base:base + 512].rearrange(
                        "(n p) d -> p n d", p=64)).then_inc(wl0, 16)
                p.g.dma_start(out=bc_str.ap().rearrange(
                    "p (n d) -> p n d", n=4),
                    in_=bc_all[base:base + 512].rearrange(
                        "(n p) d -> p n d", p=128)).then_inc(wl1, 16)
                p.g.dma_start(out=dt_str.ap().rearrange(
                    "p (n h) -> p n h", n=4),
                    in_=dt_hd[base:base + 512].rearrange(
                        "(n p) h -> p n h", p=128)).then_inc(wl1, 16)
                p.t.wait(wl1, wl1.v)
                for jt in range(4):
                    p.t.wait(dve, dve.v)
                    p.t.transpose(ptp[:, 0:128],
                                  bc_str[:, jt * 256:jt * 256 + 128],
                                  ident[:, :])
                    p.t.transpose(ptp[:, 128:256],
                                  bc_str[:, jt * 256 + 128:(jt + 1) * 256],
                                  ident[:, :]).then_inc(pe)
                    p.t.transpose(ptf[0:MYH, 0:128],
                                  dt_str[:, jt * MYH:(jt + 1) * MYH],
                                  identf[:, :]).then_inc(pe)
                    p.v.wait(pe, pe.v)
                    p.v.tensor_scalar_mul(bfm[:, jt * 128:(jt + 1) * 128],
                                          ptp[:, 0:128], bnw[:, 0:1])
                    p.v.tensor_scalar_mul(cfm[:, jt * 128:(jt + 1) * 128],
                                          ptp[:, 128:256], cnw[:, 0:1])
                    p.v.tensor_copy(dtfm[:, jt * 128:(jt + 1) * 128],
                                    ptf[0:MYH, 0:128]).then_inc(dve)
                p.s.wait(dve, dve.v)
                p.s.copy(dtfmc[:, :], dtfm[:, :]).then_inc(act)
                p.v.wait(act, act.v)
                p.v.wait(sems["sld"])
                p.v.tensor_scalar_mul(afm[:, :], dtfmc[:, :],
                                      aneg[:, 0:1]).then_inc(dve)
                p.s.wait(dve, dve.v)
                p.s.copy(afmc[:, :], afm[:, :]).then_inc(act)
                p.v.wait(act, act.v)
                for cc8 in range(8):
                    p.v.tensor_tensor_scan(
                        cum_s[:, cc8 * 64:(cc8 + 1) * 64], ones64[:, :],
                        afmc[:, cc8 * 64:(cc8 + 1) * 64], 0.0,
                        ALU.mult, ALU.add)
                p.events[-1][1].then_inc(dve)
                p.s.wait(dve, dve.v)
                p.s.activation(cumng[:, :], cum_s[:, :], AF.Copy,
                               scale=-1.0).then_inc(act)
                ecl_src = cum_s[:, :].rearrange("h (c l) -> h c l",
                                                c=8)[:, :, 63]
                p.s.copy(ecls[:, :], ecl_src).then_inc(act)
                p.v.wait(act, act.v)
                p.g.wait(dve, dve.v)
                p.g.dma_start(out=cumstr_dram.ap(),
                              in_=cum_s[:, :]).then_inc(wl0, 16)
                p.g.wait(wl0, wl0.v)
                p.g.dma_start(out=cumflat[0:1, :].rearrange(
                    "o (c h l) -> o c h l", c=8, h=MYH),
                    in_=cumstr_dram.ap().rearrange(
                        "h (c l) -> c h l", c=8)[None]).then_inc(wl0, 16)
                p.t.wait(act, act.v)
                p.t.wait(dve, dve.v)
                p.t.matmul(ps1[0:8, 0:256], ecls[:, :], e4f[:, :],
                           start=True, stop=True).then_inc(mm)
                p.v.wait(mm, mm.v)
                p.v.tensor_copy(eclev[:, :], ps1[0:8, 0:256]).then_inc(dve)
                p.g.wait(dve, dve.v)
                p.g.dma_start(out=ecl8_dram.ap(),
                              in_=eclev[:, :]).then_inc(wl0, 16)
                p.g.wait(wl0, wl0.v)
                p.g.dma_start(out=eclflat[0:1, :].rearrange(
                    "o (c d) -> o c d", c=8),
                    in_=ecl8_dram.ap()[None]).then_inc(wl0, 16)
                p.s.wait(wl0, wl0.v)
                p.s.activation(elastflat[:, :], eclflat[:, :],
                               AF.Exp).then_inc(act)
                p.t.wait(act, act.v)
            xs_sl = xs_str[:, (c % 8) * 256:(c % 8 + 1) * 256]
            p.t.wait(dve, dve.v)
            p.t.transpose(ptp[0:64, 0:128], bfm[:, cl:cl + 64],
                          ident[:, :]).then_inc(pe)
            p.v.wait(pe, pe.v)
            p.v.tensor_copy(bts[:, :], ptp[0:64, 0:128]).then_inc(dve)
            p.t.wait(dve, dve.v)
            p.t.matmul(ps1[0:64, 0:64], cfm[:, cl:cl + 64],
                       bfm[:, cl:cl + 64], start=True, stop=True).then_inc(mm)
            p.v.wait(mm, mm.v)
            p.v.tensor_mul(gtm[:, :], ps1[0:64, 0:64], ltc[:, :]).then_inc(dve)
            p.t.wait(dve, dve.v)
            p.t.matmul(ps2[0:64, 0:256], gtm[:, :], e2c[:, :], start=True,
                       stop=True)
            p.t.matmul(ps1[0:64, 0:256], onesf[:, 0:64],
                       cumflat[0:1, cs:cs + 256], start=True, stop=False)
            p.t.matmul(ps1[0:64, 0:256], cumng[:, cl:cl + 64],
                       e4f[:, :], start=False, stop=True).then_inc(mm)
            p.v.wait(mm, mm.v)
            p.v.tensor_single_scalar(dmin[:, :], ps1[0:64, 0:256], 0.0,
                                     ALU.min).then_inc(dve)
            p.s.wait(dve, dve.v)
            p.s.activation(exd[:, :], dmin[:, :], AF.Exp).then_inc(act)
            p.v.wait(act, act.v)
            p.v.tensor_mul(gm[:, :], exd[:, :], ps2[0:64, 0:256]).then_inc(dve)
            p.t.wait(dve, dve.v)
            p.t.matmul(ps1[0:64, 0:256], onesf[:, 0:64],
                       eclflat[0:1, cs:cs + 256], start=True, stop=False)
            p.t.matmul(ps1[0:64, 0:256], cumng[:, cl:cl + 64],
                       e4f[:, :], start=False, stop=True)
            p.t.matmul(ps2[0:64, 0:256], dtfm[:, cl:cl + 64], e4f[:, :],
                       start=True, stop=True).then_inc(mm)
            p.s.wait(mm, mm.v)
            p.s.activation(efb[:, :], ps1[0:64, 0:256], AF.Exp).then_inc(act)
            p.v.wait(act, act.v)
            p.v.tensor_mul(xdt1[:, :], xs_sl, ps2[0:64, 0:256]).then_inc(dve)
            p.s.wait(dve, dve.v)
            p.s.copy(xdt1c[:, :], xdt1[:, :]).then_inc(act)
            p.v.wait(act, act.v)
            p.v.tensor_mul(xdt2[:, :], xdt1c[:, :], efb[:, :]).then_inc(dve)
            p.t.wait(dve, dve.v)
            for h in range(MYH):
                p.t.matmul(ps3[0:64, h * 64:(h + 1) * 64],
                           gm[:, h * 64:(h + 1) * 64],
                           xdt1[:, h * 64:(h + 1) * 64],
                           start=True, stop=True)
            p.t.matmul(ps4[0:64, 0:256], cfm[:, cl:cl + 64], Nbf[:, :],
                       start=True, stop=True)
            p.t.matmul(ps1[0:64, 0:256], cum_s[:, cl:cl + 64],
                       e4f[:, :], start=True, stop=True).then_inc(mm)
            p.s.wait(mm, mm.v)
            p.s.activation(eplb[:, :], ps1[0:64, 0:256], AF.Exp).then_inc(act)
            p.v.wait(act, act.v)
            p.v.tensor_mul(yosb[:, :], eplb[:, :],
                           ps4[0:64, 0:256]).then_inc(dve)
            p.s.wait(dve, dve.v)
            p.s.copy(yosc[:, :], yosb[:, :]).then_inc(act)
            p.v.wait(act, act.v)
            ysl = ysb[:, (c % 8) * 256:(c % 8 + 1) * 256]
            p.v.tensor_add(ysl, yosc[:, :], ps3[0:64, 0:256]).then_inc(dve)
            p.t.wait(dve, dve.v)
            p.t.matmul(ps5[:, 0:256], ones1f[:, :],
                       elastflat[0:1, cs:cs + 256], start=True,
                       stop=True).then_inc(mm)
            p.v.wait(mm, mm.v)
            p.v.tensor_mul(Nst[:, :], Nst[:, :], ps5[:, 0:256]).then_inc(dve)
            p.s.wait(dve, dve.v)
            p.s.copy(Nsc[:, :], Nst[:, :]).then_inc(act)
            p.t.wait(dve, dve.v)
            p.t.matmul(ps5[:, 0:256], bts[:, :], xdt2[:, :], start=True,
                       stop=True).then_inc(mm)
            p.v.wait(mm, mm.v)
            p.v.wait(act, act.v)
            p.v.tensor_add(Nst[:, :], Nsc[:, :], ps5[:, 0:256]).then_inc(dve)
            p.s.wait(dve, dve.v)
            p.s.activation(Nbf[:, :], Nst[:, :], AF.Copy).then_inc(act)
            p.t.wait(act, act.v)
            if c % 8 == 7:
                base = r * T + slot * 1024 + b * 512
                p.g.wait(dve, dve.v)
                p.g.dma_start(out=y_in[base:base + 512].rearrange(
                    "(n p) d -> p n d", p=64),
                    in_=ysb.ap().rearrange("p (n d) -> p n d", n=8)
                    ).then_inc(ystore, 16)
                if c < 127:
                    p.v.wait(ystore, ystore.v)
                    p.v.memset(ysb[:, :], 0.0).then_inc(dve)
    p.g.wait(ystore, ystore.v)
    with allow_ici_all_to_all():
        p.g.collective_compute(
            "AllToAll", ALU.bypass, replica_groups=[list(range(NCORES))],
            ins=[y_in.ap().opt()], outs=[y_hd.ap().opt()]).inc_default(cc)
    p.g.wait(cc, cc.v)
    mark(4)

    wmo = hm_fm
    dma_w_fm(p, p.g, wmo, W["mb_outT"], D_INNER, D, wl0)
    dexpb = p.sbuf(tag + "_dexpb", [128, 2048], F32)
    dexr_all = p.sbuf(tag + "_dexr", [1, 2048], BF16)
    dexr = dexr_all[0:1, 0:1024]
    dexrB = dexr_all[0:1, 1024:2048]
    p.g.dma_start(out=dexr, in_=W["mb_dexp"][0:1, 0:1024]).then_inc(wl2, 16)
    p.g.dma_start(out=dexrB,
                  in_=W["mb_dexp"][0:1, 1024:2048]).then_inc(wl2, 16)
    p.t.wait(wl2, wl2.v)
    p.t.matmul(ps1[:, 0:512], ones1[:, :], dexr[0:1, 0:512], start=True,
               stop=True)
    p.t.matmul(ps2[:, 0:512], ones1[:, :], dexr[0:1, 512:1024], start=True,
               stop=True)
    p.t.matmul(ps3[:, 0:512], ones1[:, :], dexrB[0:1, 0:512], start=True,
               stop=True)
    p.t.matmul(ps4[:, 0:512], ones1[:, :], dexrB[0:1, 512:1024], start=True,
               stop=True).then_inc(mm)
    p.v.wait(mm, mm.v)
    p.v.tensor_copy(dexpb[:, 0:512], ps1[:, 0:512])
    p.v.tensor_copy(dexpb[:, 512:1024], ps2[:, 0:512])
    p.v.tensor_copy(dexpb[:, 1024:1536], ps3[:, 0:512])
    p.v.tensor_copy(dexpb[:, 1536:2048], ps4[:, 0:512]).then_inc(dve)
    p.t.wait(wl0, wl0.v)
    p.t.wait(dve, dve.v)
    mark(5)
    yt = p.sbuf(tag + "_yt", [128, 2048], BF16)
    zt = p.sbuf(tag + "_zt", [128, 2048], BF16)
    xst = p.sbuf(tag + "_xst", [128, 2048], BF16)
    g3 = p.sbuf(tag + "_g3", [128, 2048], BF16)
    gfm = p.sbuf(tag + "_gfm", [128, 2048], BF16)
    yld = p.sem(tag + "_yld")
    zld = p.sem(tag + "_zld")
    xsl = p.sem(tag + "_xsl")
    for tt in range(NTT):
        p.g.wait(dve, dve.v)
        p.g.dma_start(out=as_ap(yt).rearrange("p (blk d) -> p blk d", blk=8),
                      in_=y_hd.ap().rearrange("(blk t) d -> blk t d",
                                              blk=NCORES)
                      [:, tt * 128:(tt + 1) * 128, :].rearrange(
                          "blk p d -> p blk d")).then_inc(yld, 16)
        p.g.dma_start(out=zt[:, :],
                      in_=z_dram[tt * 128:(tt + 1) * 128, :]).then_inc(zld, 16)
        p.g.dma_start(out=as_ap(xst).rearrange("p (blk d) -> p blk d", blk=8),
                      in_=xs_in.ap().rearrange("(blk t) d -> blk t d",
                                               blk=NCORES)
                      [:, tt * 128:(tt + 1) * 128, :].rearrange(
                          "blk p d -> p blk d")).then_inc(xsl, 16)
        p.v.wait(yld, yld.v)
        p.v.wait(zld, zld.v)
        p.v.wait(xsl, xsl.v)
        p.v.tensor_mul(g3[:, :], xst[:, :], dexpb[:, :])
        p.v.tensor_add(g3[:, :], g3[:, :], yt[:, :]).then_inc(dve)
        p.s.wait(dve, dve.v)
        p.s.activation(zt[:, :], zt[:, :], AF.Silu).then_inc(act)
        p.v.wait(act, act.v)
        p.v.tensor_mul(g3[:, :], g3[:, :], zt[:, :]).then_inc(dve)
        p.t.wait(dve, dve.v)
        r = None
        for ch in range(16):
            dstp = ptp if ch < 8 else ptq
            r = p.t.transpose(dstp[:, (ch % 8) * 128:(ch % 8 + 1) * 128],
                              g3[:, ch * 128:(ch + 1) * 128], ident[:, :])
        r.then_inc(pe)
        p.v.wait(pe, pe.v)
        p.v.tensor_copy(gfm[:, 0:1024], ptp[:, :])
        p.v.tensor_copy(gfm[:, 1024:2048], ptq[:, :]).then_inc(dve)
        p.t.wait(dve, dve.v)
        r = None
        for h2 in range(2):
            for ch in range(16):
                r = p.t.matmul(ps1[:, 0:512] if h2 == 0 else ps2[:, 0:512],
                               gfm[:, ch * 128:(ch + 1) * 128],
                               wmo[:, ch * 1024 + h2 * 512:
                                   ch * 1024 + (h2 + 1) * 512],
                               start=(ch == 0), stop=(ch == 15))
        r.then_inc(mm)
        sl = xldS[tt % 2]
        if tt >= 2:
            p.q.wait(dve, dve.v - 2)
        p.q.dma_start(out=xt2[tt % 2][:, :],
                      in_=xin[tt * 128:(tt + 1) * 128, :]).then_inc(sl, 16)
        p.v.wait(sl, sl.v)
        p.v.tensor_scalar(xt2[tt % 2][:, :], xt2[tt % 2][:, :],
                          st["mu"][:, tt:tt + 1], st["rstd"][:, tt:tt + 1],
                          ALU.subtract, ALU.mult)
        p.v.wait(mm, mm.v)
        p.v.tensor_add(xt2[tt % 2][:, :][:, 0:512], xt2[tt % 2][:, 0:512],
                       ps1[:, 0:512])
        p.v.tensor_add(xt2[tt % 2][:, 512:1024], xt2[tt % 2][:, 512:1024],
                       ps2[:, 0:512]).then_inc(dve)
        p.t.wait(dve, dve.v)
        p.q.wait(dve, dve.v)
        p.q.dma_start(out=uout[tt * 128:(tt + 1) * 128, :],
                      in_=xt2[tt % 2][:, :]).then_inc(sto, 16)
    mark(6)
    return sto.v


def build_mamba_nc(dbg=False, marks=True, dbg_sel=None, marks_out=None):
    nc = mk_nc()
    p = Prog(nc)
    p.want_marks = marks
    p.marks_out = dbg if marks_out is None else marks_out
    p.dbg_sel = dbg_sel
    xin = nc.dram_tensor("xin", [T, D], F32, kind="ExternalInput").ap()
    uout = nc.dram_tensor("uout", [T, D], F32, kind="ExternalOutput").ap()
    sems = mk_sems(p)
    W = declare_mamba(nc)
    ident = load_ident(p, nc, sems)
    emit_mamba(p, nc, xin, uout, W, ident, sems)
    p.q.wait(sems["sto"])
    if dbg:
        dsem = p.sem("dbgsem")
        nd = 0
        sel = getattr(p, "dbg_sel", None)
        for k, t in p.mamba_dbg.items():
            if sel is not None and k not in sel:
                continue
            o = nc.dram_tensor("dbg_" + k, list(t.shape), t.dtype,
                               kind="ExternalOutput")
            p.q.dma_start(out=o.ap(), in_=t.ap()).then_inc(dsem, 16)
            nd += 16
        p.q.wait(dsem, nd)
    return p.finalize()


def kernel(**inputs):
    import numpy as _np
    I = {k: _np.asarray(v) for k, v in inputs.items()}
    x0 = I["L0_attn_a_x"].astype(_np.float32)
    ident = _np.eye(128, dtype=_np.float32)
    from concourse.bass_utils import run_bass_kernel_spmd
    cores = list(range(NCORES))

    def launch(nc, maps, out):
        for attempt in range(4):
            res = run_bass_kernel_spmd(nc, maps, core_ids=cores)
            sh = [_np.asarray(res.results[c][out]) for c in cores]
            if all(_np.isfinite(s).all() for s in sh):
                return sh
        return sh

    xsh = shard_tokens(x0)

    maps = []
    for c in cores:
        W = prep_mamba_host(I, c)
        W["ident"] = ident
        maps.append(dict(W, xin=xsh[c]))
    ush = launch(build_mamba_nc(), maps, "uout")

    maps = []
    for c in cores:
        W = prep_l0attn_host(I, c)
        W["ident"] = ident
        maps.append(dict(W, xin=xsh[c], uin=ush[c]))
    xsh = launch(build_l0attn_nc(), maps, "xout")

    Wmoe = prep_moe_host(I)
    Wmoe["ident"] = ident
    nc_moe = build_moe_nc()
    xsh = launch(nc_moe, [dict(Wmoe, xin=xsh[c]) for c in cores], "xout")

    Wab = prep_attnb_host(I)
    Wab["ident"] = ident
    xsh = launch(build_attnb_nc(),
                 [dict(Wab, xin=xsh[c], ab_kill=prep_attnb_kill(c))
                  for c in cores], "xout")

    xsh = launch(nc_moe, [dict(Wmoe, xin=xsh[c]) for c in cores], "xout")

    Wla = prep_l1attna_host(I)
    Wla["ident"] = ident
    xsh = launch(build_l1attna_nc(),
                 [dict(Wla, xin=xsh[c]) for c in cores], "xout")

    Wlf = prep_l1ffn_host(I)
    Wlf["ident"] = ident
    nc_lf = build_l1ffn_nc()
    xsh = launch(nc_lf, [dict(Wlf, xin=xsh[c]) for c in cores], "xout")

    Wlb = prep_l1attnb_host(I)
    Wlb["ident"] = ident
    Wlb["lb_selin"] = prep_l1attnb_sel()
    xsh = launch(build_l1attnb_nc(),
                 [dict(Wlb, xin=xsh[c]) for c in cores], "xout")

    xsh = launch(nc_lf, [dict(Wlf, xin=xsh[c]) for c in cores], "xout")

    return unshard_tokens(xsh).astype(_np.float32)
